# version 17
# baseline (speedup 1.0000x reference)
import os
import numpy as np
from contextlib import ExitStack

import concourse.bass as bass
import concourse.mybir as mybir
from concourse.bass_utils import run_bass_kernel_spmd

F32 = mybir.dt.float32
BF16 = mybir.dt.bfloat16
I32 = mybir.dt.int32
AF = mybir.ActivationFunctionType
ALU = mybir.AluOpType
AX = mybir.AxisListType

S = 2048
D = 4096
HD = 128
NH = 32
NKV = 8
FF = 16384
OWN = 1024
NT = 16
NTO = 8
KC = 32
FC = 128

DEBUG = bool(int(os.environ.get("MK_DEBUG", "0")))
TRUNC = int(os.environ.get("MK_TRUNC", "9"))
TRUNCA = int(os.environ.get("MK_TRUNCA", "9"))
LNT = int(os.environ.get("MK_LNT", "99"))

ENGS = ("tensor", "vector", "scalar", "gpsimd", "sync")


class Prog:
    def __init__(self, sems):
        self.sems = sems
        self.q = {e: [] for e in ENGS}
        self.cnt = {k: 0 for k in sems}
        self.war = {}
        self.on = True

    def wait(self, eng, sem, val=None):
        if not self.on:
            return
        v = self.cnt[sem] if val is None else val
        if v <= 0:
            return
        h = self.sems[sem]
        self.q[eng].append(lambda E, h=h, v=v: E.wait_ge(h, v))

    def do(self, eng, fn, inc=None, amt=1):
        if not self.on:
            return None
        if inc is None:
            self.q[eng].append(fn)
            return None
        self.cnt[inc] += amt
        h = self.sems[inc]

        def g(E, fn=fn, h=h, amt=amt):
            fn(E).then_inc(h, amt)

        self.q[eng].append(g)
        return self.cnt[inc]

    def dma(self, eng, sem, out_ap, in_ap):
        return self.do(eng, lambda E, o=out_ap, i=in_ap: E.dma_start(out=o, in_=i),
                       inc=sem, amt=16)

    def barrier(self):
        snap = dict(self.cnt)
        for e in ENGS:
            for s, v in snap.items():
                self.wait(e, s, v)

    def wr_wait(self, eng, key):
        for (s, v) in self.war.get(key, ()):
            self.wait(eng, s, v)

    def rd_mark(self, key, sem, val, replace=True):
        if replace:
            self.war[key] = [(sem, val)]
        else:
            self.war.setdefault(key, []).append((sem, val))

    def finalize(self, block):
        for e in ENGS:
            ops = self.q[e]
            if not ops:
                continue
            dec = getattr(block, e)

            @dec
            def _(E, ops=ops):
                for f in ops:
                    f(E)


def build_nc():
    nc = bass.Bass("TRN2", target_bir_lowering=False)

    def inp(name, shape, dt):
        return nc.dram_tensor(name, shape, dt, kind="ExternalInput")

    xf = inp("xf", [S, D], F32)
    xo = inp("xo", [OWN, D], F32)
    qc = inp("qc", [D, KC * 128], I32)
    qsc = inp("qsc", [NH, KC * 128], F32)
    kc_ = inp("kc", [NKV * HD, KC * 128], I32)
    ksc = inp("ksc", [NKV, KC * 128], F32)
    vc = inp("vc", [NKV * HD, KC * 128], I32)
    vsc = inp("vsc", [NKV, KC * 128], F32)
    oc_ = inp("oc", [D, KC * 128], I32)
    osc = inp("osc", [KC, KC * 128], F32)
    w1 = inp("w1", [FF, D], F32)
    b1 = inp("b1", [128, FC], F32)
    w2 = inp("w2", [FF, D], F32)
    b2 = inp("b2", [128, KC], F32)
    cosq = inp("cosq", [128, OWN], F32)
    sinq = inp("sinq", [128, OWN], F32)
    cosk = inp("cosk", [128, S], F32)
    sink = inp("sink", [128, S], F32)
    maskc = inp("maskc", [128, NTO * 256], F32)
    identf = inp("identf", [128, 128], F32)

    out = nc.dram_tensor("out", [OWN, D], F32, kind="ExternalOutput")
    dkind = "ExternalOutput" if DEBUG else "Internal"
    qdr = nc.dram_tensor("qdr", [D, OWN], BF16, kind=dkind)
    x1dr = nc.dram_tensor("x1dr", [OWN, D], F32, kind=dkind)
    hdr = nc.dram_tensor("hdr", [FF, OWN], BF16, kind=dkind)
    otdr = nc.dram_tensor("otdr", [OWN, D], BF16, kind=dkind)
    if DEBUG:
        kdbg = nc.dram_tensor("kdbg", [128, NKV * S], BF16, kind="ExternalOutput")
        vdbg = nc.dram_tensor("vdbg", [128, NT * 1024], BF16, kind="ExternalOutput")
        aodbg = nc.dram_tensor("aodbg", [128, NH * 1024], BF16, kind="ExternalOutput")

    ctx = ExitStack()
    ent = ctx.enter_context
    DMA_SEMS = (["tbl", "scd0", "scd1", "qst", "x1s"] +
                [f"{b}{i}" for b in ("cb", "xt", "qh", "ots", "xo", "otb",
                                     "w1", "hs", "w2", "x1t", "os") for i in range(2)] +
                [f"hb{i}" for i in range(4)])
    sems = {n: ent(nc.semaphore(n)) for n in
            ("sP", "sV", "sC") + tuple(DMA_SEMS)}
    p = Prog(sems)

    ident = ent(nc.sbuf_tensor("ident", [128, 128], BF16, allow_name_mangling=True))
    epsv = ent(nc.sbuf_tensor("epsv", [128, 1], F32, allow_name_mangling=True))
    bnbuf = ent(nc.sbuf_tensor("bnbuf", [128, 48], F32, allow_name_mangling=True))
    mvv = ent(nc.sbuf_tensor("mvv", [128, 2], F32, allow_name_mangling=True))
    stdv = ent(nc.sbuf_tensor("stdv", [128, 1], F32, allow_name_mangling=True))
    rstdv = ent(nc.sbuf_tensor("rstdv", [128, 1], F32, allow_name_mangling=True))

    p.dma("gpsimd", "tbl", ident[:, :], identf.ap())
    p.do("vector", lambda E: E.memset(epsv[:, :1], 1e-5), inc="sV")

    def emit_ln_apply(x_ap, xn_ap, ready_waits=()):
        for (sname, v) in ready_waits:
            p.wait("vector", sname, v)
        p.wr_wait("vector", "bnbuf")
        for i in range(7):
            p.do("vector", lambda E, i=i, x=x_ap: E.bn_stats(
                bnbuf[:, i * 6:(i + 1) * 6], x[:, i * 512:(i + 1) * 512]))
        m = p.do("vector", lambda E, x=x_ap: E.bn_stats(
            bnbuf[:, 42:48], x[:, 3584:4096]), inc="sV")
        p.wait("vector", "sV", m)
        p.wr_wait("vector", "mvv")
        m2 = p.do("vector", lambda E: E.bn_aggr(mvv[:, :2], bnbuf[:, :48]), inc="sV")
        p.rd_mark("bnbuf", "sV", m2)
        p.wait("scalar", "sV", m2)
        p.wr_wait("scalar", "stdv")
        m3 = p.do("scalar", lambda E: E.activation(
            stdv[:, :1], mvv[:, 1:2], AF.Sqrt, bias=epsv[:, :1]), inc="sC")
        p.wait("vector", "sC", m3)
        p.wr_wait("vector", "rstdv")
        m4 = p.do("vector", lambda E: E.reciprocal(rstdv[:, :1], stdv[:, :1]), inc="sV")
        p.rd_mark("stdv", "sV", m4)
        p.wait("vector", "sV", m4)
        m5 = p.do("vector", lambda E, x=x_ap, xn=xn_ap: E.tensor_scalar(
            xn, x, mvv[:, 0:1], rstdv[:, :1], ALU.subtract, ALU.mult), inc="sV")
        p.rd_mark("mvv", "sV", m5)
        p.rd_mark("rstdv", "sV", m5)
        return m5

    def emit_transpose_tile(src_sb, src_m, trp, dest_sb, cstride, coff, dest_key,
                            nchunks=32, evac_scalar=False):
        last = None
        for grp in range((nchunks + 7) // 8):
            tb = grp % 2
            p.wr_wait("tensor", ("trp", id(trp[tb])))
            p.wait("tensor", "sV", src_m)
            n_in = min(8, nchunks - grp * 8)
            mt = None
            for i in range(n_in):
                c = grp * 8 + i
                mt = p.do("tensor", lambda E, c=c, i=i, tb=tb: E.transpose(
                    trp[tb][:, i * 128:(i + 1) * 128],
                    src_sb[:, c * 128:(c + 1) * 128], ident[:, :]),
                    inc="sP" if i == n_in - 1 else None)
            eng = "scalar" if evac_scalar else "vector"
            esem = "sC" if evac_scalar else "sV"
            p.wait(eng, "sP", mt)
            p.wr_wait(eng, dest_key)
            row = dest_sb.shape[1]
            off = coff + (grp * 8) * cstride
            dest_ap = bass.AP(dest_sb, off, [[row, 128], [cstride, n_in], [1, 128]])
            if evac_scalar:
                last = p.do("scalar", lambda E, d=dest_ap, tb=tb, n=n_in: E.activation(
                    d, trp[tb][:, 0:n * 128], AF.Copy), inc="sC")
            else:
                last = p.do("vector", lambda E, d=dest_ap, tb=tb, n=n_in: E.tensor_copy(
                    d, trp[tb][:, 0:n * 128]), inc="sV")
            p.rd_mark(("trp", id(trp[tb])), esem, last)
        return last, esem

    def emit_dequant_block(codes, scales, W, ocb, par, cb, sc, wb):
        p.wr_wait("gpsimd", ("cb", id(cb[par])))
        mc = p.dma("gpsimd", f"cb{par}",
                   cb[par][:, :],
                   bass.AP(codes, ocb * (128 * KC * 128), [[KC * 128, 128], [1, KC * 128]]))
        HW = KC * 64
        p.wait("vector", f"cb{par}", mc)
        p.wr_wait("vector", ("wb", id(wb[par])))
        mw = None
        for hf in range(2):
            p.wr_wait("sync", ("sc", id(sc[hf])))
            ms = p.dma("sync", f"scd{hf}", sc[hf][:, :],
                       bass.AP(scales, ocb * (KC * 128) + hf * HW, [[0, 128], [1, HW]]))
            p.wait("vector", f"scd{hf}", ms)
            mw = p.do("vector", lambda E, par=par, hf=hf: E.scalar_tensor_tensor(
                wb[par][:, hf * HW:(hf + 1) * HW],
                cb[par][:, hf * HW:(hf + 1) * HW], -8.0, sc[hf][:, :],
                ALU.add, ALU.mult), inc="sV")
            p.rd_mark(("sc", id(sc[hf])), "sV", mw)
        p.rd_mark(("cb", id(cb[par])), "sV", mw)
        return mw

    ctxAB = ExitStack()
    K_sb = ctxAB.enter_context(nc.sbuf_tensor("K_sb", [128, NKV * S], BF16, allow_name_mangling=True))
    V_tm = ctxAB.enter_context(nc.sbuf_tensor("V_tm", [128, NT * 1024], BF16, allow_name_mangling=True))

    with ExitStack() as actx:
        aent = actx.enter_context
        cosk_sb = aent(nc.sbuf_tensor("cosk_sb", [128, S], BF16, allow_name_mangling=True))
        sink_sb = aent(nc.sbuf_tensor("sink_sb", [128, S], BF16, allow_name_mangling=True))
        cosq_sb = aent(nc.sbuf_tensor("cosq_sb", [128, OWN], BF16, allow_name_mangling=True))
        sinq_sb = aent(nc.sbuf_tensor("sinq_sb", [128, OWN], BF16, allow_name_mangling=True))
        p.dma("gpsimd", "tbl", cosk_sb[:, :], cosk.ap())
        p.dma("gpsimd", "tbl", sink_sb[:, :], sink.ap())
        p.dma("gpsimd", "tbl", cosq_sb[:, :], cosq.ap())
        mtab = p.dma("gpsimd", "tbl", sinq_sb[:, :], sinq.ap())

        xnh = aent(nc.sbuf_tensor("xnh", [128, KC * 1024], BF16, allow_name_mangling=True))
        trp = [aent(nc.psum_tensor(f"trpA{i}", [128, 1024], BF16, allow_name_mangling=True)) for i in range(2)]
        pacc = [aent(nc.psum_tensor(f"paccA{i}", [128, 1024], F32, allow_name_mangling=True)) for i in range(2)]
        p.wait("vector", "tbl", mtab)

        def emit_rope(ub1, ub2, src, dst_sb, dst_off, n, cos_sb, sin_sb, tab_off):
            cos0 = cos_sb[0:64, tab_off:tab_off + n]
            cos1 = cos_sb[64:128, tab_off:tab_off + n]
            sin0 = sin_sb[0:64, tab_off:tab_off + n]
            sin1 = sin_sb[64:128, tab_off:tab_off + n]
            p.do("vector", lambda E: E.tensor_mul(ub1[0:64, 0:n], src[0:64, 0:n], cos0))
            p.do("vector", lambda E: E.tensor_mul(ub1[64:128, 0:n], src[64:128, 0:n], cos1))
            p.do("vector", lambda E: E.tensor_mul(ub2[0:64, 0:n], src[64:128, 0:n], sin1))
            mu = p.do("vector", lambda E: E.tensor_mul(ub2[64:128, 0:n], src[0:64, 0:n], sin0), inc="sV")
            p.wait("vector", "sV", mu)
            d0 = dst_sb[0:64, dst_off:dst_off + n]
            d1 = dst_sb[64:128, dst_off:dst_off + n]
            p.do("vector", lambda E, d0=d0: E.tensor_tensor(d0, ub1[0:64, 0:n], ub2[0:64, 0:n], ALU.subtract))
            mr = p.do("vector", lambda E, d1=d1: E.tensor_tensor(d1, ub1[64:128, 0:n], ub2[64:128, 0:n], ALU.add), inc="sV")
            return mr

        def emit_ln_sub(src_dram, tiles):
            tiles = tiles[:LNT]
            with ExitStack() as lctx:
                lent = lctx.enter_context
                xtile = [lent(nc.sbuf_tensor(f"xtile{i}", [128, D], F32, allow_name_mangling=True)) for i in range(2)]
                xn_tm = [lent(nc.sbuf_tensor(f"xn_tm{i}", [128, D], BF16, allow_name_mangling=True)) for i in range(2)]
                pend = None
                for tl, t in enumerate(tiles):
                    xpar = tl % 2
                    p.wr_wait("sync", ("xtile", id(xtile[xpar])))
                    mx = p.dma("sync", f"xt{xpar}", xtile[xpar][:, :],
                               src_dram.ap()[t * 128:(t + 1) * 128, :])
                    p.wr_wait("vector", ("xn_tm", xpar))
                    mln = emit_ln_apply(xtile[xpar][:, :], xn_tm[xpar][:, :], [(f"xt{xpar}", mx)])
                    p.rd_mark(("xtile", id(xtile[xpar])), "sV", mln)
                    if pend is not None:
                        ptl, ppar, pmln = pend
                        emit_transpose_tile(xn_tm[ppar], pmln, trp, xnh, 1024, ptl * 128, "xnh", evac_scalar=True)
                        p.rd_mark(("xn_tm", ppar), "sP", p.cnt["sP"])
                    pend = (tl, xpar, mln)
                ptl, ppar, pmln = pend
                emit_transpose_tile(xn_tm[ppar], pmln, trp, xnh, 1024, ptl * 128, "xnh")
                p.rd_mark(("xn_tm", ppar), "sP", p.cnt["sP"])
            p.barrier()

        def emit_proj_sub(kinds):
            with ExitStack() as kctx:
                kent = kctx.enter_context
                cb = [kent(nc.sbuf_tensor(f"acb{i}", [128, KC * 128], BF16, allow_name_mangling=True)) for i in range(2)]
                sc = [kent(nc.sbuf_tensor(f"asc{i}", [128, KC * 64], F32, allow_name_mangling=True)) for i in range(2)]
                wb = [kent(nc.sbuf_tensor(f"awb{i}", [128, KC * 128], BF16, allow_name_mangling=True)) for i in range(2)]
                kst = kent(nc.sbuf_tensor("kst", [128, 1024], BF16, allow_name_mangling=True))
                ub1 = kent(nc.sbuf_tensor("ub1", [128, 1024], BF16, allow_name_mangling=True))
                ub2 = kent(nc.sbuf_tensor("ub2", [128, 1024], BF16, allow_name_mangling=True))
                vst = kent(nc.sbuf_tensor("vst", [128, 1024], BF16, allow_name_mangling=True))
                items = [(kind, codes, scales, W, ocb, hh)
                         for kind, codes, scales, W, nocb, hh in kinds
                         for ocb in range(nocb)]
                mws = {}
                if items:
                    _, c0, s0, W0, o0, _ = items[0]
                    mws[0] = emit_dequant_block(c0, s0, W0, o0, 0, cb, sc, wb)
                for it in range(len(items)):
                    kind, codes, scales, W, ocb, hh = items[it]
                    par = it % 2
                    if it + 1 < len(items):
                        _, c1, s1, W1, o1, _ = items[it + 1]
                        mws[it + 1] = emit_dequant_block(c1, s1, W1, o1, (it + 1) % 2, cb, sc, wb)
                    if True:
                        mw = mws.pop(it)
                        p.wait("tensor", "sV", mw)
                        p.wr_wait("tensor", ("pacc", par))
                        mp = None
                        for tg in range(2):
                            for kch in range(KC):
                                mp = p.do("tensor", lambda E, par=par, tg=tg, kch=kch: E.matmul(
                                    pacc[par][:, tg * 512:(tg + 1) * 512],
                                    wb[par][:, kch * 128:(kch + 1) * 128],
                                    xnh[:, kch * 1024 + tg * 512: kch * 1024 + (tg + 1) * 512],
                                    start=(kch == 0), stop=(kch == KC - 1)),
                                    inc="sP" if (tg == 1 and kch == KC - 1) else None)
                        p.rd_mark(("wb", id(wb[par])), "sP", mp)
                        p.wait("scalar", "sP", mp)
                        stg = vst if kind == "v" else kst
                        skey = "vst_stg" if kind == "v" else "kst_stg"
                        p.wr_wait("scalar", skey)
                        me = p.do("scalar", lambda E, stg=stg, par=par: E.activation(
                            stg[:, :], pacc[par][:, :], AF.Copy), inc="sC")
                        p.rd_mark(("pacc", par), "sC", me)
                        if kind == "k":
                            p.wait("vector", "sC", me)
                            mr = emit_rope(ub1, ub2, kst, K_sb, ocb * S + hh * 1024, 1024,
                                           cosk_sb, sink_sb, hh * 1024)
                            p.rd_mark("kst_stg", "sV", mr)
                        elif kind == "v":
                            tb = ocb % 2
                            p.wr_wait("tensor", ("trp", id(trp[tb])))
                            p.wait("tensor", "sC", me)
                            mt = None
                            for i in range(8):
                                mt = p.do("tensor", lambda E, i=i, tb=tb: E.transpose(
                                    trp[tb][:, i * 128:(i + 1) * 128],
                                    vst[:, i * 128:(i + 1) * 128], ident[:, :]),
                                    inc="sP" if i == 7 else None)
                            p.rd_mark("vst_stg", "sP", mt)
                            p.wait("vector", "sP", mt)
                            dest = bass.AP(V_tm, (hh * 8) * 1024 + ocb * 128,
                                           [[NT * 1024, 128], [1024, 8], [1, 128]])
                            mv2 = p.do("vector", lambda E, d=dest, tb=tb: E.tensor_copy(
                                d, trp[tb][:, :]), inc="sV")
                            p.rd_mark(("trp", id(trp[tb])), "sV", mv2)
                        else:
                            p.wait("vector", "sC", me)
                            p.wr_wait("vector", "vst_stg")
                            mr = emit_rope(ub1, ub2, kst, vst, 0, 1024,
                                           cosq_sb, sinq_sb, 0)
                            p.rd_mark("kst_stg", "sV", mr)
                            p.wait("sync", "sV", mr)
                            mq = p.dma("sync", "qst",
                                       qdr.ap()[ocb * 128:(ocb + 1) * 128, :], vst[:, :])
                            p.rd_mark("vst_stg", "qst", mq)
            p.barrier()

        for hh in range(2):
            p.on = TRUNCA >= 1 + hh * 2
            emit_ln_sub(xf, [hh * 8 + tl for tl in range(8)])
            p.on = TRUNCA >= 2 + hh * 2
            emit_proj_sub([("k", kc_, ksc, NKV * HD, NKV, hh),
                           ("v", vc, vsc, NKV * HD, NKV, hh)])

        p.on = TRUNCA >= 5
        emit_ln_sub(xo, list(range(NTO)))
        p.on = TRUNCA >= 6
        emit_proj_sub([("q", qc, qsc, D, NH, 0)])

    if DEBUG:
        p.barrier()
        p.dma("sync", "qst", kdbg.ap()[:, :], K_sb[:, :])
        p.dma("sync", "qst", vdbg.ap()[:, :], V_tm[:, :])
        p.barrier()

    p.on = TRUNC >= 2 and TRUNCA >= 9
    p.barrier()
    ctxAO = ExitStack()
    AO = ctxAO.enter_context(nc.sbuf_tensor("AO", [128, NH * 1024], BF16, side="right", allow_name_mangling=True))
    with ExitStack() as bctx:
        bent = bctx.enter_context
        qh = [bent(nc.sbuf_tensor(f"qh{i}", [128, 1024], BF16, allow_name_mangling=True)) for i in range(2)]
        mask_sb = bent(nc.sbuf_tensor("mask_sb", [128, NTO * 256], BF16, allow_name_mangling=True))
        Pe_sb = [bent(nc.sbuf_tensor(f"Pe{i}", [128, 2048], BF16, allow_name_mangling=True)) for i in range(2)]
        Pt_sb = [bent(nc.sbuf_tensor(f"Pt{i}", [128, 2048], BF16, allow_name_mangling=True)) for i in range(3)]
        ksmp = bent(nc.sbuf_tensor("ksmp", [128, 128], BF16, allow_name_mangling=True))
        biasv = bent(nc.sbuf_tensor("biasv", [128, 2], F32, allow_name_mangling=True))
        zv = bent(nc.sbuf_tensor("zv", [128, 2], F32, allow_name_mangling=True))
        rz = bent(nc.sbuf_tensor("rz", [128, 2], F32, allow_name_mangling=True))
        Sps = bent(nc.psum_tensor("Sps", [128, 2048], F32, allow_name_mangling=True))
        smpps = bent(nc.psum_tensor("smpps", [128, 512], F32, allow_name_mangling=True))
        avps = bent(nc.psum_tensor("avps", [128, 512], F32, allow_name_mangling=True))
        trpB = [bent(nc.psum_tensor(f"trpB{i}", [128, 1024], BF16, allow_name_mangling=True)) for i in range(2)]

        p.dma("gpsimd", "tbl", mask_sb[:, :], maskc.ap())
        p.dma("sync", "qh0", qh[0][:, :], qdr.ap()[0:128, :])
        p.barrier()

        units = [(h, s) for h in range(NH) for s in range(NTO)]
        prev = None
        prev2 = []
        for u in range(len(units) + 2):
            cur = None
            if u < len(units):
                h, s = units[u]
                par = u % 2
                g = h // 4
                ext = 2 * s + 2
                if s == 0 and h + 1 < NH:
                    p.wr_wait("sync", ("qh", (h + 1) % 2))
                    mqh = p.dma("sync", f"qh{(h + 1) % 2}", qh[(h + 1) % 2][:, :],
                                qdr.ap()[(h + 1) * 128:(h + 2) * 128, :])
                    p.rd_mark(("qh_ld", (h + 1) % 2), f"qh{(h + 1) % 2}", mqh)
                if s == 0 and h % 4 == 0:
                    p.wr_wait("vector", "ksmp")
                    msm = p.do("vector", lambda E, g=g: E.tensor_copy(
                        ksmp[:, :], bass.AP(K_sb, g * S, [[NKV * S, 128], [16, 128]])),
                        inc="sV")
                    p.rd_mark("ksmp_w", "sV", msm)
                p.wait("tensor", "sV", p.war["ksmp_w"][0][1])
                if s == 0:
                    p.wr_wait("tensor", ("qh_ld", h % 2))
                p.wr_wait("tensor", "smp")
                msp = p.do("tensor", lambda E, h=h, s=s: E.matmul(
                    smpps[:, 0:128],
                    qh[h % 2][:, s * 128:(s + 1) * 128],
                    ksmp[:, :], start=True, stop=True), inc="sP")
                p.rd_mark("ksmp", "sP", msp)
                p.wait("vector", "sP", msp)
                p.wr_wait("vector", ("biasv", par))
                mb = p.do("vector", lambda E, par=par: E.reduce_max(
                    biasv[:, par:par + 1], smpps[:, 0:128],
                    axis=AX.X, negate=True), inc="sV")
                p.rd_mark("smp", "sV", mb)
                if prev is not None:
                    _ph, _ps, _ppar, _pext, _pmex = prev
                    p.wait("vector", "sC", _pmex)
                    p.wr_wait("vector", ("rz", _ppar))
                    _mrz = p.do("vector", lambda E, _ppar=_ppar: E.reciprocal(
                        rz[:, _ppar:_ppar + 1], zv[:, _ppar:_ppar + 1]), inc="sV")
                    p.rd_mark(("zv", _ppar), "sV", _mrz)
                    p.wait("vector", "sV", _mrz)
                    _pmmk = p.do("vector", lambda E, _pext=_pext, _ppar=_ppar: E.tensor_scalar(
                        Pe_sb[_ppar][:, 0:_pext * 128], Pe_sb[_ppar][:, 0:_pext * 128],
                        rz[:, _ppar:_ppar + 1], None, ALU.mult), inc="sV")
                    p.rd_mark(("rz", _ppar), "sV", _pmmk)
                    prev = (_ph, _ps, _ppar, _pext, _pmmk)
                soff = (par * 1024) if ext <= 8 else 0
                if ext <= 8:
                    p.wr_wait("tensor", ("Sh", par))
                else:
                    p.wr_wait("tensor", ("Sh", 0))
                    p.wr_wait("tensor", ("Sh", 1))
                nsl = (ext * 128 + 511) // 512
                for sl in range(nsl):
                    w = min(512, ext * 128 - sl * 512)
                    p.do("tensor", lambda E, h=h, s=s, g=g, sl=sl, w=w, soff=soff: E.matmul(
                        Sps[:, soff + sl * 512: soff + sl * 512 + w],
                        qh[h % 2][:, s * 128:(s + 1) * 128],
                        K_sb[:, g * S + sl * 512: g * S + sl * 512 + w],
                        start=True, stop=False))
                msc = p.do("tensor", lambda E, ext=ext, s=s, soff=soff: E.matmul(
                    Sps[:, soff + (ext - 2) * 128: soff + ext * 128],
                    ident[:, :],
                    mask_sb[:, s * 256:(s + 1) * 256],
                    start=False, stop=True), inc="sP")
                if s == NTO - 1:
                    p.rd_mark(("qh", h % 2), "sP", msc)
                p.wait("scalar", "sP", msc)
                p.wait("scalar", "sV", mb)
                p.wr_wait("scalar", ("Pe", par))
                p.wr_wait("scalar", ("zv", par))
                mex = p.do("scalar", lambda E, ext=ext, par=par, soff=soff: E.activation(
                    Pe_sb[par][:, 0:ext * 128], Sps[:, soff:soff + ext * 128], AF.Exp,
                    bias=biasv[:, par:par + 1],
                    accum_out=zv[:, par:par + 1]), inc="sC")
                if ext <= 8:
                    p.rd_mark(("Sh", par), "sC", mex)
                else:
                    p.rd_mark(("Sh", 0), "sC", mex)
                    p.rd_mark(("Sh", 1), "sC", mex)
                p.rd_mark(("biasv", par), "sC", mex)
                cur = (h, s, par, ext, mex)

            if prev is not None:
                ph, ps_, ppar, pext, pmmk = prev
                pu = u - 1
                if u >= len(units):
                    p.wait("vector", "sC", pmmk)
                    p.wr_wait("vector", ("rz", ppar))
                    _mrz = p.do("vector", lambda E, ppar=ppar: E.reciprocal(
                        rz[:, ppar:ppar + 1], zv[:, ppar:ppar + 1]), inc="sV")
                    p.rd_mark(("zv", ppar), "sV", _mrz)
                    p.wait("vector", "sV", _mrz)
                    pmmk = p.do("vector", lambda E, pext=pext, ppar=ppar: E.tensor_scalar(
                        Pe_sb[ppar][:, 0:pext * 128], Pe_sb[ppar][:, 0:pext * 128],
                        rz[:, ppar:ppar + 1], None, ALU.mult), inc="sV")
                    p.rd_mark(("rz", ppar), "sV", pmmk)
                pt3 = pu % 3
                mev = None
                for grp in range((pext + 7) // 8):
                    tb = grp % 2
                    n_in = min(8, pext - grp * 8)
                    p.wr_wait("tensor", ("trpB", tb))
                    p.wait("tensor", "sV", pmmk)
                    mt = None
                    for i in range(n_in):
                        ti = grp * 8 + i
                        mt = p.do("tensor", lambda E, ti=ti, i=i, tb=tb, ppar=ppar: E.transpose(
                            trpB[tb][:, i * 128:(i + 1) * 128],
                            Pe_sb[ppar][:, ti * 128:(ti + 1) * 128], ident[:, :]),
                            inc="sP" if i == n_in - 1 else None)
                    p.wait("vector", "sP", mt)
                    p.wr_wait("vector", ("Pt", pt3))
                    mev = p.do("vector", lambda E, tb=tb, pt3=pt3, grp=grp, n=n_in:
                               E.tensor_copy(Pt_sb[pt3][:, grp * 1024: grp * 1024 + n * 128],
                                             trpB[tb][:, 0:n * 128]), inc="sV")
                    p.rd_mark(("trpB", tb), "sV", mev)
                p.rd_mark(("Pe", ppar), "sP", p.cnt["sP"])
                prev2 = prev2 + [(ph, ps_, pext, pt3, mev)]
            if len(prev2) > (1 if u < len(units) + 1 else 0):
                ph2, ps2, pext2, pt32, mev2 = prev2.pop(0)
                pg2 = ph2 // 4
                p.wait("tensor", "sV", mev2)
                p.wr_wait("tensor", "av")
                mav = None
                for ti in range(pext2):
                    mav = p.do("tensor", lambda E, ti=ti, pg2=pg2, pt32=pt32, pext2=pext2: E.matmul(
                        avps[:, 0:128],
                        V_tm[:, ti * 1024 + pg2 * 128: ti * 1024 + (pg2 + 1) * 128],
                        Pt_sb[pt32][:, ti * 128:(ti + 1) * 128],
                        start=(ti == 0), stop=(ti == pext2 - 1)),
                        inc="sP" if ti == pext2 - 1 else None)
                p.rd_mark(("Pt", pt32), "sP", mav)
                p.wait("scalar", "sP", mav)
                mao = p.do("scalar", lambda E, ph2=ph2, ps2=ps2: E.activation(
                    AO[:, ph2 * 1024 + ps2 * 128: ph2 * 1024 + (ps2 + 1) * 128],
                    avps[:, 0:128], AF.Copy), inc="sC")
                p.rd_mark("av", "sC", mao)
            prev = cur

    if DEBUG:
        p.barrier()
        p.dma("sync", "qst", aodbg.ap()[:, :], AO[:, :])
        p.barrier()

    ctxAB.close()

    p.on = TRUNC >= 3
    p.barrier()
    ctxX3 = ExitStack()
    xn3 = ctxX3.enter_context(nc.sbuf_tensor("xn3", [128, KC * 1024], BF16, allow_name_mangling=True))
    with ExitStack() as cctx:
        cent = cctx.enter_context
        oacc = [cent(nc.psum_tensor(f"oacc{i}", [128, 512], F32, allow_name_mangling=True)) for i in range(2)]
        trpC = [cent(nc.psum_tensor(f"trpC{i}", [128, 1024], BF16, allow_name_mangling=True)) for i in range(2)]
        octx = ExitStack()
        cb = [octx.enter_context(nc.sbuf_tensor(f"ccb{i}", [128, KC * 128], BF16, allow_name_mangling=True)) for i in range(2)]
        sc = [octx.enter_context(nc.sbuf_tensor(f"csc{i}", [128, KC * 64], F32, allow_name_mangling=True)) for i in range(2)]
        wb = [octx.enter_context(nc.sbuf_tensor(f"cwb{i}", [128, KC * 128], BF16, allow_name_mangling=True)) for i in range(2)]
        ostg = [octx.enter_context(nc.sbuf_tensor(f"ostg{i}", [128, 512], BF16, allow_name_mangling=True)) for i in range(2)]
        otst = [octx.enter_context(nc.sbuf_tensor(f"otst{i}", [128, 512], BF16, allow_name_mangling=True)) for i in range(2)]

        mws = {0: emit_dequant_block(oc_, osc, D, 0, 0, cb, sc, wb)}
        for ocb in range(KC):
            parw = ocb % 2
            if ocb + 1 < KC:
                mws[ocb + 1] = emit_dequant_block(oc_, osc, D, ocb + 1, (ocb + 1) % 2, cb, sc, wb)
            mw = mws.pop(ocb)
            p.wait("tensor", "sV", mw)
            for tg in range(2):
                par = tg
                p.wr_wait("tensor", ("oacc", par))
                mp = None
                for ac in range(KC):
                    mp = p.do("tensor", lambda E, par=par, ac=ac, tg=tg, parw=parw: E.matmul(
                        oacc[par][:, :],
                        wb[parw][:, ac * 128:(ac + 1) * 128],
                        AO[:, ac * 1024 + tg * 512: ac * 1024 + (tg + 1) * 512],
                        start=(ac == 0), stop=(ac == KC - 1)),
                        inc="sP" if ac == KC - 1 else None)
                if tg == 1:
                    p.rd_mark(("wb", id(wb[parw])), "sP", mp)
                p.wait("scalar", "sP", mp)
                p.wr_wait("scalar", ("ostg", par))
                me = p.do("scalar", lambda E, par=par: E.activation(
                    ostg[par][:, :], oacc[par][:, :], AF.Copy), inc="sC")
                p.rd_mark(("oacc", par), "sC", me)
                tb = par
                p.wr_wait("tensor", ("trp", id(trpC[tb])))
                p.wait("tensor", "sC", me)
                mt = None
                for i in range(4):
                    mt = p.do("tensor", lambda E, i=i, tb=tb, par=par: E.transpose(
                        trpC[tb][:, i * 128:(i + 1) * 128],
                        ostg[par][:, i * 128:(i + 1) * 128], ident[:, :]),
                        inc="sP" if i == 3 else None)
                p.rd_mark(("ostg", par), "sP", mt)
                p.wait("vector", "sP", mt)
                p.wr_wait("vector", ("otst", par))
                mv2 = p.do("vector", lambda E, tb=tb, par=par: E.tensor_copy(
                    otst[par][:, :], trpC[tb][:, 0:512]), inc="sV")
                p.rd_mark(("trp", id(trpC[tb])), "sV", mv2)
                p.wait("sync", "sV", mv2)
                dest = bass.AP(otdr, (tg * 4 * 128) * D + ocb * 128,
                               [[D, 128], [128 * D, 4], [1, 128]])
                mo2 = p.dma("sync", f"ots{par}", dest, otst[par][:, :].rearrange(
                    "p (t c) -> p t c", c=128))
                p.rd_mark(("otst", par), f"ots{par}", mo2)

        ctxAO.close()
        octx.close()
        p.barrier()
        xot = [cent(nc.sbuf_tensor(f"xot{i}", [128, D], F32, allow_name_mangling=True)) for i in range(2)]
        otb = [cent(nc.sbuf_tensor(f"otb{i}", [128, D], BF16, allow_name_mangling=True)) for i in range(2)]
        t2b = cent(nc.sbuf_tensor("t2b", [128, D], BF16, allow_name_mangling=True))
        x1b = cent(nc.sbuf_tensor("x1b", [128, D], F32, allow_name_mangling=True))
        xn3tm = cent(nc.sbuf_tensor("xn3tm", [128, D], BF16, allow_name_mangling=True))
        for tl in range(NTO):
            xpar = tl % 2
            p.wr_wait("sync", ("xot", xpar))
            mx = p.dma("sync", f"xo{xpar}", xot[xpar][:, :], xo.ap()[tl * 128:(tl + 1) * 128, :])
            p.wr_wait("sync", ("otb", xpar))
            mot = p.dma("sync", f"otb{xpar}", otb[xpar][:, :], otdr.ap()[tl * 128:(tl + 1) * 128, :])
            p.wr_wait("vector", "t2b")
            mln2 = emit_ln_apply(otb[xpar][:, :], t2b[:, :], [(f"otb{xpar}", mot)])
            p.rd_mark(("otb", xpar), "sV", mln2)
            p.wait("vector", f"xo{xpar}", mx)
            p.wr_wait("vector", "x1b")
            mx1 = p.do("vector", lambda E, xpar=xpar: E.tensor_tensor(
                x1b[:, :], xot[xpar][:, :], t2b[:, :], ALU.add), inc="sV")
            p.rd_mark(("xot", xpar), "sV", mx1)
            p.rd_mark("t2b", "sV", mx1)
            p.wait("sync", "sV", mx1)
            ms1 = p.dma("sync", "x1s", x1dr.ap()[tl * 128:(tl + 1) * 128, :], x1b[:, :])
            p.wr_wait("vector", "xn3tm")
            mln3 = emit_ln_apply(x1b[:, :], xn3tm[:, :])
            p.rd_mark("x1b", "sV", mln3)
            p.rd_mark("x1b", "x1s", ms1, replace=False)
            emit_transpose_tile(xn3tm, mln3, trpC, xn3, 1024, tl * 128, "xn3", evac_scalar=True)
            p.rd_mark("xn3tm", "sP", p.cnt["sP"])

    p.on = TRUNC >= 4
    p.barrier()
    with ExitStack() as dctx:
        dent = dctx.enter_context
        b1_sb = dent(nc.sbuf_tensor("b1_sb", [128, FC], F32, allow_name_mangling=True))
        w1b = [dent(nc.sbuf_tensor(f"w1b{i}", [128, KC * 128], BF16, allow_name_mangling=True)) for i in range(2)]
        hstg = [dent(nc.sbuf_tensor(f"hstg{i}", [128, 512], BF16, allow_name_mangling=True)) for i in range(2)]
        hps = [dent(nc.psum_tensor(f"hps{i}", [128, 512], F32, allow_name_mangling=True)) for i in range(2)]
        mb1 = p.dma("sync", "tbl", b1_sb[:, :], b1.ap())
        p.wait("scalar", "tbl", mb1)
        for fc in range(FC):
            parw = fc % 2
            p.wr_wait("gpsimd", ("w1b", parw))
            mwd = p.dma("gpsimd", f"w1{parw}",
                        w1b[parw][:, :],
                        bass.AP(w1, fc * (128 * D), [[D, 128], [1, D]]))
            p.wait("tensor", f"w1{parw}", mwd)
            for tg in range(2):
                par = tg
                p.wr_wait("tensor", ("hps", par))
                mp = None
                for kch in range(KC):
                    mp = p.do("tensor", lambda E, par=par, parw=parw, kch=kch, tg=tg: E.matmul(
                        hps[par][:, :],
                        w1b[parw][:, kch * 128:(kch + 1) * 128],
                        xn3[:, kch * 1024 + tg * 512: kch * 1024 + (tg + 1) * 512],
                        start=(kch == 0), stop=(kch == KC - 1)),
                        inc="sP" if kch == KC - 1 else None)
                if tg == 1:
                    p.rd_mark(("w1b", parw), "sP", mp)
                p.wait("scalar", "sP", mp)
                p.wr_wait("scalar", ("hstg", par))
                mg = p.do("scalar", lambda E, par=par, fc=fc: E.activation(
                    hstg[par][:, :], hps[par][:, :], AF.Gelu_apprx_tanh,
                    bias=b1_sb[:, fc:fc + 1]), inc="sC")
                p.rd_mark(("hps", par), "sC", mg)
                p.wait("sync", "sC", mg)
                mh = p.dma("sync", f"hs{par}",
                           hdr.ap()[fc * 128:(fc + 1) * 128, tg * 512:(tg + 1) * 512],
                           hstg[par][:, :])
                p.rd_mark(("hstg", par), f"hs{par}", mh)
    ctxX3.close()

    p.on = TRUNC >= 5
    p.barrier()
    FT = ent(nc.sbuf_tensor("FT", [128, NTO * D], BF16, allow_name_mangling=True))
    bnF = ent(nc.sbuf_tensor("bnF", [128, NTO * KC * 6], F32, allow_name_mangling=True))
    with ExitStack() as ectx:
        eent = ectx.enter_context
        b2_sb = eent(nc.sbuf_tensor("b2_sb", [128, KC], F32, allow_name_mangling=True))
        w2b = [eent(nc.sbuf_tensor(f"w2b{i}", [128, 4 * 512], BF16, allow_name_mangling=True)) for i in range(2)]
        hb = [eent(nc.sbuf_tensor(f"hb{i}", [128, 1024], BF16, allow_name_mangling=True)) for i in range(4)]
        festg = eent(nc.sbuf_tensor("festg", [128, D], BF16, allow_name_mangling=True))
        mb2 = p.dma("sync", "tbl", b2_sb[:, :], b2.ap())
        prev_trpE_done = None
        for dg in range(8):
            with ExitStack() as psctx:
                acc = [psctx.enter_context(nc.psum_tensor(f"eacc{dg}_{i}", [128, 1024], F32, allow_name_mangling=True))
                       for i in range(4)]
                mwd = None
                for fc in range(FC):
                    if fc % 4 == 0:
                        parw = (fc // 4) % 2
                        p.wr_wait("gpsimd", ("w2b", parw))
                        mwd = p.dma("gpsimd", f"w2{parw}",
                                    w2b[parw][:, :],
                                    bass.AP(w2, ((dg * 32 + fc // 4) * 128) * 2048,
                                            [[2048, 128], [1, 2048]]))
                        wsem = f"w2{parw}"
                    parh = fc % 4
                    p.wr_wait("sync", ("hb", parh))
                    mhl = p.dma("sync", f"hb{parh}", hb[parh][:, :],
                                hdr.ap()[fc * 128:(fc + 1) * 128, :])
                    p.wait("tensor", wsem, mwd)
                    p.wait("tensor", f"hb{parh}", mhl)
                    if fc == 0 and prev_trpE_done is not None:
                        p.wait("tensor", "sV", prev_trpE_done)
                    mp = None
                    for tg in range(2):
                        for dc in range(4):
                            mp = p.do("tensor", lambda E, dc=dc, tg=tg, parw=parw, fc=fc, parh=parh: E.matmul(
                                acc[dc][:, tg * 512:(tg + 1) * 512],
                                w2b[parw][:, (fc % 4) * 512 + dc * 128: (fc % 4) * 512 + (dc + 1) * 128],
                                hb[parh][:, tg * 512:(tg + 1) * 512],
                                start=(fc == 0), stop=(fc == FC - 1)),
                                inc="sP" if (dc == 3 and tg == 1) else None)
                    p.rd_mark(("hb", parh), "sP", mp)
                    if fc % 4 == 3:
                        p.rd_mark(("w2b", parw), "sP", mp)
                p.wait("vector", "sP", mp)
                p.wait("vector", "tbl", mb2)
                p.wr_wait("vector", "festg")
                mev = None
                for dc in range(4):
                    for tg in range(2):
                        mev = p.do("vector", lambda E, dc=dc, tg=tg, dg=dg: E.tensor_scalar(
                            festg[:, (dc * 2 + tg) * 512:(dc * 2 + tg + 1) * 512],
                            acc[dc][:, tg * 512:(tg + 1) * 512],
                            b2_sb[:, dg * 4 + dc: dg * 4 + dc + 1], None, ALU.add),
                            inc="sV")
            with ExitStack() as psctx2:
                trpE = [psctx2.enter_context(
                    nc.psum_tensor(f"trpE{dg}_{i}", [128, 1024], BF16, allow_name_mangling=True)) for i in range(2)]
                mv2 = None
                for dc in range(4):
                    tb = dc % 2
                    p.wr_wait("tensor", ("trpE", tb))
                    p.wait("tensor", "sV", mev)
                    mt = None
                    for i in range(8):
                        tg = i // 4
                        i2 = i % 4
                        mt = p.do("tensor", lambda E, dc=dc, tg=tg, i2=i2, i=i, tb=tb: E.transpose(
                            trpE[tb][:, i * 128:(i + 1) * 128],
                            festg[:, (dc * 2 + tg) * 512 + i2 * 128:
                                  (dc * 2 + tg) * 512 + (i2 + 1) * 128],
                            ident[:, :]), inc="sP" if i == 7 else None)
                    if dc == 3:
                        p.rd_mark("festg", "sP", mt)
                    p.wait("vector", "sP", mt)
                    dest = bass.AP(FT, (dg * 4 + dc) * 128, [[NTO * D, 128], [D, 8], [1, 128]])
                    mv2 = p.do("vector", lambda E, d=dest, tb=tb: E.tensor_copy(
                        d, trpE[tb][:, :]), inc="sV")
                    p.rd_mark(("trpE", tb), "sV", mv2)
                    ch = dg * 4 + dc
                    for tl in range(NTO):
                        p.do("vector", lambda E, tl=tl, ch=ch: E.bn_stats(
                            bnF[:, (tl * KC + ch) * 6:(tl * KC + ch) * 6 + 6],
                            FT[:, tl * D + ch * 128: tl * D + (ch + 1) * 128]))
                    mv2 = p.do("vector", lambda E: E.engine_nop(), inc="sV")
                prev_trpE_done = mv2

    p.on = TRUNC >= 6
    p.barrier()
    with ExitStack() as fctx:
        fent = fctx.enter_context
        x1t = [fent(nc.sbuf_tensor(f"x1t{i}", [128, D], F32, allow_name_mangling=True)) for i in range(2)]
        t4b = fent(nc.sbuf_tensor("t4b", [128, D], BF16, allow_name_mangling=True))
        outb = [fent(nc.sbuf_tensor(f"outb{i}", [128, D], F32, allow_name_mangling=True)) for i in range(2)]
        for tl in range(NTO):
            xpar = tl % 2
            p.wr_wait("sync", ("x1t", xpar))
            mx = p.dma("sync", f"x1t{xpar}", x1t[xpar][:, :], x1dr.ap()[tl * 128:(tl + 1) * 128, :])
            p.wr_wait("vector", "t4b")
            p.wr_wait("vector", "mvv")
            m2 = p.do("vector", lambda E, tl=tl: E.bn_aggr(
                mvv[:, :2], bnF[:, tl * KC * 6:(tl + 1) * KC * 6]), inc="sV")
            p.wait("scalar", "sV", m2)
            p.wr_wait("scalar", "stdv")
            m3 = p.do("scalar", lambda E: E.activation(
                stdv[:, :1], mvv[:, 1:2], AF.Sqrt, bias=epsv[:, :1]), inc="sC")
            p.wait("vector", "sC", m3)
            p.wr_wait("vector", "rstdv")
            m4 = p.do("vector", lambda E: E.reciprocal(rstdv[:, :1], stdv[:, :1]), inc="sV")
            p.rd_mark("stdv", "sV", m4)
            p.wait("vector", "sV", m4)
            mln4 = p.do("vector", lambda E, tl=tl: E.tensor_scalar(
                t4b[:, :], FT[:, tl * D:(tl + 1) * D], mvv[:, 0:1], rstdv[:, :1],
                ALU.subtract, ALU.mult), inc="sV")
            p.rd_mark("mvv", "sV", mln4)
            p.rd_mark("rstdv", "sV", mln4)
            p.wait("vector", f"x1t{xpar}", mx)
            p.wr_wait("vector", ("outb", xpar))
            mo = p.do("vector", lambda E, xpar=xpar: E.tensor_tensor(
                outb[xpar][:, :], x1t[xpar][:, :], t4b[:, :], ALU.add), inc="sV")
            p.rd_mark(("x1t", xpar), "sV", mo)
            p.rd_mark("t4b", "sV", mo)
            p.wait("sync", "sV", mo)
            mst = p.dma("sync", f"os{xpar}", out.ap()[tl * 128:(tl + 1) * 128, :], outb[xpar][:, :])
            p.rd_mark(("outb", xpar), f"os{xpar}", mst)

    p.on = True
    for sname in p.cnt:
        p.wait("sync", sname)
        p.wait("gpsimd", sname)

    block = ctx.enter_context(nc.Block())
    p.finalize(block)
    ctx.close()
    return nc


def _rope_perm(width):
    nh = width // HD
    perm = []
    for h in range(nh):
        base = h * HD
        perm.extend([base + 2 * i for i in range(64)])
        perm.extend([base + 2 * i + 1 for i in range(64)])
    return np.array(perm, dtype=np.int64)


def _host_tables():
    pos = np.arange(S, dtype=np.float64)
    inv = 1.0 / (10000.0 ** (np.arange(0, HD, 2, dtype=np.float64) / HD))
    f = pos[:, None] * inv[None, :]
    cos = np.cos(f).T.astype(np.float32)
    sin = np.sin(f).T.astype(np.float32)
    return cos, sin


def kernel(L0_attn_hidden_states, L0_attn_q_codes, L0_attn_q_scales,
           L0_attn_k_codes, L0_attn_k_scales, L0_attn_v_codes, L0_attn_v_scales,
           L0_attn_o_codes, L0_attn_o_scales, L0_ffn_inter_w, L0_ffn_inter_b,
           L0_ffn_output_w, L0_ffn_output_b):
    x = np.ascontiguousarray(np.asarray(L0_attn_hidden_states, dtype=np.float32))
    B = x.shape[0]
    permq = _rope_perm(D)
    permk = _rope_perm(NKV * HD)
    def codes_layout(arr, W):
        return np.ascontiguousarray(
            arr.reshape(KC, 128, W // 128, 128).transpose(2, 1, 0, 3).reshape(W, KC * 128))

    def scales_layout(arr, W):
        return np.ascontiguousarray(
            arr.reshape(KC, W // 128, 128).transpose(1, 0, 2).reshape(W // 128, KC * 128))

    qcod = codes_layout(np.asarray(L0_attn_q_codes, dtype=np.int32)[:, permq], D)
    qsc = scales_layout(np.asarray(L0_attn_q_scales, dtype=np.float32)[:, permq], D)
    kcod = codes_layout(np.asarray(L0_attn_k_codes, dtype=np.int32)[:, permk], NKV * HD)
    ksc = scales_layout(np.asarray(L0_attn_k_scales, dtype=np.float32)[:, permk], NKV * HD)
    vcod = codes_layout(np.asarray(L0_attn_v_codes, dtype=np.int32), NKV * HD)
    vsc = scales_layout(np.asarray(L0_attn_v_scales, dtype=np.float32), NKV * HD)
    ocod = codes_layout(np.asarray(L0_attn_o_codes, dtype=np.int32), D)
    osc = scales_layout(np.asarray(L0_attn_o_scales, dtype=np.float32), D)
    w1 = np.ascontiguousarray(np.asarray(L0_ffn_inter_w, dtype=np.float32)
                              .reshape(KC, 128, FC, 128).transpose(2, 1, 0, 3).reshape(FF, D))
    b1 = np.ascontiguousarray(np.asarray(L0_ffn_inter_b, dtype=np.float32).reshape(FC, 128).T)
    w2 = np.ascontiguousarray(np.asarray(L0_ffn_output_w, dtype=np.float32)
                              .reshape(32, 4, 128, 8, 512).transpose(3, 0, 2, 1, 4).reshape(FF, D))
    b2 = np.ascontiguousarray(np.asarray(L0_ffn_output_b, dtype=np.float32).reshape(KC, 128).T)
    cos, sin = _host_tables()
    ident = np.eye(128, dtype=np.float32)

    tri = np.tril(np.ones((128, 128), dtype=np.float32))
    zeros = np.zeros((128, 128), dtype=np.float32)
    ones = np.ones((128, 128), dtype=np.float32)

    in_maps = []
    for c in range(8):
        b = c // 2
        j = c % 2
        own_tiles = [2 * s + j for s in range(NTO)]
        xb = x[b]
        xown = np.ascontiguousarray(xb.reshape(NT, 128, D)[own_tiles].reshape(OWN, D))
        own_pos = np.concatenate([np.arange(t * 128, (t + 1) * 128) for t in own_tiles])
        cq = np.ascontiguousarray(np.vstack([cos[:, own_pos], cos[:, own_pos]]) * np.float32(HD ** -0.5))
        sq = np.ascontiguousarray(np.vstack([sin[:, own_pos], sin[:, own_pos]]) * np.float32(HD ** -0.5))
        NEG = np.float32(-1e30)
        tri_add = np.where(tri > 0, np.float32(0.0), NEG)
        mk = np.empty((128, NTO * 256), dtype=np.float32)
        for s in range(NTO):
            if j == 0:
                mk[:, s * 256:s * 256 + 128] = tri_add
                mk[:, s * 256 + 128:(s + 1) * 256] = NEG
            else:
                mk[:, s * 256:s * 256 + 128] = 0.0
                mk[:, s * 256 + 128:(s + 1) * 256] = tri_add
        in_maps.append({
            "xf": xb, "xo": xown,
            "qc": qcod, "qsc": qsc, "kc": kcod, "ksc": ksc,
            "vc": vcod, "vsc": vsc, "oc": ocod, "osc": osc,
            "w1": w1, "b1": b1, "w2": w2, "b2": b2,
            "cosq": cq, "sinq": sq,
            "cosk": np.ascontiguousarray(np.vstack([cos, cos])),
            "sink": np.ascontiguousarray(np.vstack([sin, sin])),
            "maskc": mk, "identf": ident,
        })

    ncores = int(os.environ.get("MK_CORES", "8"))
    nc = build_nc()
    res = run_bass_kernel_spmd(nc, in_maps[:ncores], core_ids=list(range(ncores)))

    outx = np.zeros((B, S, D), dtype=np.float32)
    for c in range(ncores):
        b = c // 2
        j = c % 2
        own_tiles = [2 * s + j for s in range(NTO)]
        o = np.asarray(res.results[c]["out"], dtype=np.float32).reshape(NTO, 128, D)
        view = outx[b].reshape(NT, 128, D)
        for i, t in enumerate(own_tiles):
            view[t] = o[i]
    return outx


# version 18
# speedup vs baseline: 1.1476x; 1.1476x over previous
import os
import numpy as np
from contextlib import ExitStack

import concourse.bass as bass
import concourse.mybir as mybir
from concourse.bass_utils import run_bass_kernel_spmd

F32 = mybir.dt.float32
BF16 = mybir.dt.bfloat16
I32 = mybir.dt.int32
AF = mybir.ActivationFunctionType
ALU = mybir.AluOpType
AX = mybir.AxisListType

S = 2048
D = 4096
HD = 128
NH = 32
NKV = 8
FF = 16384
OWN = 1024
NT = 16
NTO = 8
KC = 32
FC = 128

DEBUG = bool(int(os.environ.get("MK_DEBUG", "0")))
TRUNC = int(os.environ.get("MK_TRUNC", "9"))
TRUNCA = int(os.environ.get("MK_TRUNCA", "9"))
LNT = int(os.environ.get("MK_LNT", "99"))

ENGS = ("tensor", "vector", "scalar", "gpsimd", "sync")


class Prog:
    def __init__(self, sems):
        self.sems = sems
        self.q = {e: [] for e in ENGS}
        self.cnt = {k: 0 for k in sems}
        self.war = {}
        self.on = True

    def wait(self, eng, sem, val=None):
        if not self.on:
            return
        v = self.cnt[sem] if val is None else val
        if v <= 0:
            return
        h = self.sems[sem]
        self.q[eng].append(lambda E, h=h, v=v: E.wait_ge(h, v))

    def do(self, eng, fn, inc=None, amt=1):
        if not self.on:
            return None
        if inc is None:
            self.q[eng].append(fn)
            return None
        self.cnt[inc] += amt
        h = self.sems[inc]

        def g(E, fn=fn, h=h, amt=amt):
            fn(E).then_inc(h, amt)

        self.q[eng].append(g)
        return self.cnt[inc]

    def dma(self, eng, sem, out_ap, in_ap):
        return self.do(eng, lambda E, o=out_ap, i=in_ap: E.dma_start(out=o, in_=i),
                       inc=sem, amt=16)

    def barrier(self):
        snap = dict(self.cnt)
        for e in ENGS:
            for s, v in snap.items():
                self.wait(e, s, v)

    def wr_wait(self, eng, key):
        for (s, v) in self.war.get(key, ()):
            self.wait(eng, s, v)

    def rd_mark(self, key, sem, val, replace=True):
        if replace:
            self.war[key] = [(sem, val)]
        else:
            self.war.setdefault(key, []).append((sem, val))

    def finalize(self, block):
        for e in ENGS:
            ops = self.q[e]
            if not ops:
                continue
            dec = getattr(block, e)

            @dec
            def _(E, ops=ops):
                for f in ops:
                    f(E)


def build_nc():
    nc = bass.Bass("TRN2", target_bir_lowering=False)

    def inp(name, shape, dt):
        return nc.dram_tensor(name, shape, dt, kind="ExternalInput")

    xf = inp("xf", [S, D], F32)
    xo = inp("xo", [OWN, D], F32)
    qc = inp("qc", [D, KC * 128], I32)
    qsc = inp("qsc", [NH, KC * 128], F32)
    kc_ = inp("kc", [NKV * HD, KC * 128], I32)
    ksc = inp("ksc", [NKV, KC * 128], F32)
    vc = inp("vc", [NKV * HD, KC * 128], I32)
    vsc = inp("vsc", [NKV, KC * 128], F32)
    oc_ = inp("oc", [D, KC * 128], I32)
    osc = inp("osc", [KC, KC * 128], F32)
    w1 = inp("w1", [FF, D], F32)
    b1 = inp("b1", [128, FC], F32)
    w2 = inp("w2", [FF, D], F32)
    b2 = inp("b2", [128, KC], F32)
    cosq = inp("cosq", [128, OWN], F32)
    sinq = inp("sinq", [128, OWN], F32)
    cosk = inp("cosk", [128, S], F32)
    sink = inp("sink", [128, S], F32)
    maskc = inp("maskc", [128, NTO * 256], F32)
    identf = inp("identf", [128, 128], F32)

    out = nc.dram_tensor("out", [OWN, D], F32, kind="ExternalOutput")
    dkind = "ExternalOutput" if DEBUG else "Internal"
    qdr = nc.dram_tensor("qdr", [D, OWN], BF16, kind=dkind)
    x1dr = nc.dram_tensor("x1dr", [OWN, D], F32, kind=dkind)
    hdr = nc.dram_tensor("hdr", [FF, OWN], BF16, kind=dkind)
    otdr = nc.dram_tensor("otdr", [OWN, D], BF16, kind=dkind)
    if DEBUG:
        kdbg = nc.dram_tensor("kdbg", [128, NKV * S], BF16, kind="ExternalOutput")
        vdbg = nc.dram_tensor("vdbg", [128, NT * 1024], BF16, kind="ExternalOutput")
        aodbg = nc.dram_tensor("aodbg", [128, NH * 1024], BF16, kind="ExternalOutput")

    ctx = ExitStack()
    ent = ctx.enter_context
    DMA_SEMS = (["tbl", "scd0", "scd1", "qst", "x1s"] +
                [f"{b}{i}" for b in ("cb", "xt", "qh", "ots", "xo", "otb",
                                     "w1", "hs", "w2", "x1t", "os") for i in range(2)] +
                [f"hb{i}" for i in range(4)])
    sems = {n: ent(nc.semaphore(n)) for n in
            ("sP", "sV", "sC") + tuple(DMA_SEMS)}
    p = Prog(sems)

    ident = ent(nc.sbuf_tensor("ident", [128, 128], BF16, allow_name_mangling=True))
    epsv = ent(nc.sbuf_tensor("epsv", [128, 1], F32, allow_name_mangling=True))
    bnbuf = ent(nc.sbuf_tensor("bnbuf", [128, 48], F32, allow_name_mangling=True))
    mvv = ent(nc.sbuf_tensor("mvv", [128, 2], F32, allow_name_mangling=True))
    stdv = ent(nc.sbuf_tensor("stdv", [128, 1], F32, allow_name_mangling=True))
    rstdv = ent(nc.sbuf_tensor("rstdv", [128, 1], F32, allow_name_mangling=True))

    p.dma("gpsimd", "tbl", ident[:, :], identf.ap())
    p.do("vector", lambda E: E.memset(epsv[:, :1], 1e-5), inc="sV")

    def emit_ln_apply(x_ap, xn_ap, ready_waits=()):
        for (sname, v) in ready_waits:
            p.wait("vector", sname, v)
        p.wr_wait("vector", "bnbuf")
        for i in range(7):
            p.do("vector", lambda E, i=i, x=x_ap: E.bn_stats(
                bnbuf[:, i * 6:(i + 1) * 6], x[:, i * 512:(i + 1) * 512]))
        m = p.do("vector", lambda E, x=x_ap: E.bn_stats(
            bnbuf[:, 42:48], x[:, 3584:4096]), inc="sV")
        p.wait("vector", "sV", m)
        p.wr_wait("vector", "mvv")
        m2 = p.do("vector", lambda E: E.bn_aggr(mvv[:, :2], bnbuf[:, :48]), inc="sV")
        p.rd_mark("bnbuf", "sV", m2)
        p.wait("scalar", "sV", m2)
        p.wr_wait("scalar", "stdv")
        m3 = p.do("scalar", lambda E: E.activation(
            stdv[:, :1], mvv[:, 1:2], AF.Sqrt, bias=epsv[:, :1]), inc="sC")
        p.wait("vector", "sC", m3)
        p.wr_wait("vector", "rstdv")
        m4 = p.do("vector", lambda E: E.reciprocal(rstdv[:, :1], stdv[:, :1]), inc="sV")
        p.rd_mark("stdv", "sV", m4)
        p.wait("vector", "sV", m4)
        m5 = p.do("vector", lambda E, x=x_ap, xn=xn_ap: E.tensor_scalar(
            xn, x, mvv[:, 0:1], rstdv[:, :1], ALU.subtract, ALU.mult), inc="sV")
        p.rd_mark("mvv", "sV", m5)
        p.rd_mark("rstdv", "sV", m5)
        return m5

    def emit_transpose_tile(src_sb, src_m, trp, dest_sb, cstride, coff, dest_key,
                            nchunks=32, evac_scalar=False):
        last = None
        for grp in range((nchunks + 7) // 8):
            tb = grp % 2
            p.wr_wait("tensor", ("trp", id(trp[tb])))
            p.wait("tensor", "sV", src_m)
            n_in = min(8, nchunks - grp * 8)
            mt = None
            for i in range(n_in):
                c = grp * 8 + i
                mt = p.do("tensor", lambda E, c=c, i=i, tb=tb: E.transpose(
                    trp[tb][:, i * 128:(i + 1) * 128],
                    src_sb[:, c * 128:(c + 1) * 128], ident[:, :]),
                    inc="sP" if i == n_in - 1 else None)
            eng = "scalar" if evac_scalar else "vector"
            esem = "sC" if evac_scalar else "sV"
            p.wait(eng, "sP", mt)
            p.wr_wait(eng, dest_key)
            row = dest_sb.shape[1]
            off = coff + (grp * 8) * cstride
            dest_ap = bass.AP(dest_sb, off, [[row, 128], [cstride, n_in], [1, 128]])
            if evac_scalar:
                last = p.do("scalar", lambda E, d=dest_ap, tb=tb, n=n_in: E.activation(
                    d, trp[tb][:, 0:n * 128], AF.Copy), inc="sC")
            else:
                last = p.do("vector", lambda E, d=dest_ap, tb=tb, n=n_in: E.tensor_copy(
                    d, trp[tb][:, 0:n * 128]), inc="sV")
            p.rd_mark(("trp", id(trp[tb])), esem, last)
        return last, esem

    def emit_dequant_block(codes, scales, W, ocb, par, cb, sc, wb):
        p.wr_wait("gpsimd", ("cb", id(cb[par])))
        mc = p.dma("gpsimd", f"cb{par}",
                   cb[par][:, :],
                   bass.AP(codes, ocb * (128 * KC * 128), [[KC * 128, 128], [1, KC * 128]]))
        HW = KC * 64
        p.wait("vector", f"cb{par}", mc)
        p.wr_wait("vector", ("wb", id(wb[par])))
        mw = None
        for hf in range(2):
            p.wr_wait("sync", ("sc", id(sc[hf])))
            ms = p.dma("sync", f"scd{hf}", sc[hf][:, :],
                       bass.AP(scales, ocb * (KC * 128) + hf * HW, [[0, 128], [1, HW]]))
            p.wait("vector", f"scd{hf}", ms)
            mw = p.do("vector", lambda E, par=par, hf=hf: E.scalar_tensor_tensor(
                wb[par][:, hf * HW:(hf + 1) * HW],
                cb[par][:, hf * HW:(hf + 1) * HW], -8.0, sc[hf][:, :],
                ALU.add, ALU.mult), inc="sV")
            p.rd_mark(("sc", id(sc[hf])), "sV", mw)
        p.rd_mark(("cb", id(cb[par])), "sV", mw)
        return mw

    ctxAB = ExitStack()
    K_sb = ctxAB.enter_context(nc.sbuf_tensor("K_sb", [128, NKV * S], BF16, allow_name_mangling=True))
    V_tm = ctxAB.enter_context(nc.sbuf_tensor("V_tm", [128, NT * 1024], BF16, allow_name_mangling=True))

    with ExitStack() as actx:
        aent = actx.enter_context
        cosk_sb = aent(nc.sbuf_tensor("cosk_sb", [128, S], BF16, allow_name_mangling=True))
        sink_sb = aent(nc.sbuf_tensor("sink_sb", [128, S], BF16, allow_name_mangling=True))
        cosq_sb = aent(nc.sbuf_tensor("cosq_sb", [128, OWN], BF16, allow_name_mangling=True))
        sinq_sb = aent(nc.sbuf_tensor("sinq_sb", [128, OWN], BF16, allow_name_mangling=True))
        p.dma("gpsimd", "tbl", cosk_sb[:, :], cosk.ap())
        p.dma("gpsimd", "tbl", sink_sb[:, :], sink.ap())
        p.dma("gpsimd", "tbl", cosq_sb[:, :], cosq.ap())
        mtab = p.dma("gpsimd", "tbl", sinq_sb[:, :], sinq.ap())

        xnh = aent(nc.sbuf_tensor("xnh", [128, KC * 1024], BF16, allow_name_mangling=True))
        trp = [aent(nc.psum_tensor(f"trpA{i}", [128, 1024], BF16, allow_name_mangling=True)) for i in range(2)]
        pacc = [aent(nc.psum_tensor(f"paccA{i}", [128, 1024], F32, allow_name_mangling=True)) for i in range(2)]
        p.wait("vector", "tbl", mtab)

        def emit_rope(ub1, ub2, src, dst_sb, dst_off, n, cos_sb, sin_sb, tab_off):
            cos0 = cos_sb[0:64, tab_off:tab_off + n]
            cos1 = cos_sb[64:128, tab_off:tab_off + n]
            sin0 = sin_sb[0:64, tab_off:tab_off + n]
            sin1 = sin_sb[64:128, tab_off:tab_off + n]
            p.do("vector", lambda E: E.tensor_mul(ub1[0:64, 0:n], src[0:64, 0:n], cos0))
            p.do("vector", lambda E: E.tensor_mul(ub1[64:128, 0:n], src[64:128, 0:n], cos1))
            p.do("vector", lambda E: E.tensor_mul(ub2[0:64, 0:n], src[64:128, 0:n], sin1))
            mu = p.do("vector", lambda E: E.tensor_mul(ub2[64:128, 0:n], src[0:64, 0:n], sin0), inc="sV")
            p.wait("vector", "sV", mu)
            d0 = dst_sb[0:64, dst_off:dst_off + n]
            d1 = dst_sb[64:128, dst_off:dst_off + n]
            p.do("vector", lambda E, d0=d0: E.tensor_tensor(d0, ub1[0:64, 0:n], ub2[0:64, 0:n], ALU.subtract))
            mr = p.do("vector", lambda E, d1=d1: E.tensor_tensor(d1, ub1[64:128, 0:n], ub2[64:128, 0:n], ALU.add), inc="sV")
            return mr

        def emit_ln_sub(src_dram, tiles):
            tiles = tiles[:LNT]
            with ExitStack() as lctx:
                lent = lctx.enter_context
                xtile = [lent(nc.sbuf_tensor(f"xtile{i}", [128, D], F32, allow_name_mangling=True)) for i in range(2)]
                xn_tm = [lent(nc.sbuf_tensor(f"xn_tm{i}", [128, D], BF16, allow_name_mangling=True)) for i in range(2)]
                pend = None
                for tl, t in enumerate(tiles):
                    xpar = tl % 2
                    p.wr_wait("sync", ("xtile", id(xtile[xpar])))
                    mx = p.dma("sync", f"xt{xpar}", xtile[xpar][:, :],
                               src_dram.ap()[t * 128:(t + 1) * 128, :])
                    p.wr_wait("vector", ("xn_tm", xpar))
                    mln = emit_ln_apply(xtile[xpar][:, :], xn_tm[xpar][:, :], [(f"xt{xpar}", mx)])
                    p.rd_mark(("xtile", id(xtile[xpar])), "sV", mln)
                    if pend is not None:
                        ptl, ppar, pmln = pend
                        emit_transpose_tile(xn_tm[ppar], pmln, trp, xnh, 1024, ptl * 128, "xnh", evac_scalar=True)
                        p.rd_mark(("xn_tm", ppar), "sP", p.cnt["sP"])
                    pend = (tl, xpar, mln)
                ptl, ppar, pmln = pend
                emit_transpose_tile(xn_tm[ppar], pmln, trp, xnh, 1024, ptl * 128, "xnh")
                p.rd_mark(("xn_tm", ppar), "sP", p.cnt["sP"])
            p.barrier()

        def emit_proj_sub(kinds):
            with ExitStack() as kctx:
                kent = kctx.enter_context
                cb = [kent(nc.sbuf_tensor(f"acb{i}", [128, KC * 128], BF16, allow_name_mangling=True)) for i in range(2)]
                sc = [kent(nc.sbuf_tensor(f"asc{i}", [128, KC * 64], F32, allow_name_mangling=True)) for i in range(2)]
                wb = [kent(nc.sbuf_tensor(f"awb{i}", [128, KC * 128], BF16, allow_name_mangling=True)) for i in range(2)]
                kst = kent(nc.sbuf_tensor("kst", [128, 1024], BF16, allow_name_mangling=True))
                ub1 = kent(nc.sbuf_tensor("ub1", [128, 1024], BF16, allow_name_mangling=True))
                ub2 = kent(nc.sbuf_tensor("ub2", [128, 1024], BF16, allow_name_mangling=True))
                vst = kent(nc.sbuf_tensor("vst", [128, 1024], BF16, allow_name_mangling=True))
                items = [(kind, codes, scales, W, ocb, hh)
                         for kind, codes, scales, W, nocb, hh in kinds
                         for ocb in range(nocb)]
                mws = {}
                if items:
                    _, c0, s0, W0, o0, _ = items[0]
                    mws[0] = emit_dequant_block(c0, s0, W0, o0, 0, cb, sc, wb)
                for it in range(len(items)):
                    kind, codes, scales, W, ocb, hh = items[it]
                    par = it % 2
                    if it + 1 < len(items):
                        _, c1, s1, W1, o1, _ = items[it + 1]
                        mws[it + 1] = emit_dequant_block(c1, s1, W1, o1, (it + 1) % 2, cb, sc, wb)
                    if True:
                        mw = mws.pop(it)
                        p.wait("tensor", "sV", mw)
                        p.wr_wait("tensor", ("pacc", par))
                        mp = None
                        for tg in range(2):
                            for kch in range(KC):
                                mp = p.do("tensor", lambda E, par=par, tg=tg, kch=kch: E.matmul(
                                    pacc[par][:, tg * 512:(tg + 1) * 512],
                                    wb[par][:, kch * 128:(kch + 1) * 128],
                                    xnh[:, kch * 1024 + tg * 512: kch * 1024 + (tg + 1) * 512],
                                    start=(kch == 0), stop=(kch == KC - 1)),
                                    inc="sP" if (tg == 1 and kch == KC - 1) else None)
                        p.rd_mark(("wb", id(wb[par])), "sP", mp)
                        p.wait("scalar", "sP", mp)
                        stg = vst if kind == "v" else kst
                        skey = "vst_stg" if kind == "v" else "kst_stg"
                        p.wr_wait("scalar", skey)
                        me = p.do("scalar", lambda E, stg=stg, par=par: E.activation(
                            stg[:, :], pacc[par][:, :], AF.Copy), inc="sC")
                        p.rd_mark(("pacc", par), "sC", me)
                        if kind == "k":
                            p.wait("vector", "sC", me)
                            mr = emit_rope(ub1, ub2, kst, K_sb, ocb * S + hh * 1024, 1024,
                                           cosk_sb, sink_sb, hh * 1024)
                            p.rd_mark("kst_stg", "sV", mr)
                        elif kind == "v":
                            tb = ocb % 2
                            p.wr_wait("tensor", ("trp", id(trp[tb])))
                            p.wait("tensor", "sC", me)
                            mt = None
                            for i in range(8):
                                mt = p.do("tensor", lambda E, i=i, tb=tb: E.transpose(
                                    trp[tb][:, i * 128:(i + 1) * 128],
                                    vst[:, i * 128:(i + 1) * 128], ident[:, :]),
                                    inc="sP" if i == 7 else None)
                            p.rd_mark("vst_stg", "sP", mt)
                            p.wait("vector", "sP", mt)
                            dest = bass.AP(V_tm, (hh * 8) * 1024 + ocb * 128,
                                           [[NT * 1024, 128], [1024, 8], [1, 128]])
                            mv2 = p.do("vector", lambda E, d=dest, tb=tb: E.tensor_copy(
                                d, trp[tb][:, :]), inc="sV")
                            p.rd_mark(("trp", id(trp[tb])), "sV", mv2)
                        else:
                            p.wait("vector", "sC", me)
                            p.wr_wait("vector", "vst_stg")
                            mr = emit_rope(ub1, ub2, kst, vst, 0, 1024,
                                           cosq_sb, sinq_sb, 0)
                            p.rd_mark("kst_stg", "sV", mr)
                            p.wait("sync", "sV", mr)
                            mq = p.dma("sync", "qst",
                                       qdr.ap()[ocb * 128:(ocb + 1) * 128, :], vst[:, :])
                            p.rd_mark("vst_stg", "qst", mq)
            p.barrier()

        for hh in range(2):
            p.on = TRUNCA >= 1 + hh * 2
            emit_ln_sub(xf, [hh * 8 + tl for tl in range(8)])
            p.on = TRUNCA >= 2 + hh * 2
            emit_proj_sub([("k", kc_, ksc, NKV * HD, NKV, hh),
                           ("v", vc, vsc, NKV * HD, NKV, hh)])

        p.on = TRUNCA >= 5
        emit_ln_sub(xo, list(range(NTO)))
        p.on = TRUNCA >= 6
        emit_proj_sub([("q", qc, qsc, D, NH, 0)])

    if DEBUG:
        p.barrier()
        p.dma("sync", "qst", kdbg.ap()[:, :], K_sb[:, :])
        p.dma("sync", "qst", vdbg.ap()[:, :], V_tm[:, :])
        p.barrier()

    p.on = TRUNC >= 2 and TRUNCA >= 9
    p.barrier()
    ctxAO = ExitStack()
    AO = ctxAO.enter_context(nc.sbuf_tensor("AO", [128, NH * 1024], BF16, side="right", allow_name_mangling=True))
    with ExitStack() as bctx:
        bent = bctx.enter_context
        qh = [bent(nc.sbuf_tensor(f"qh{i}", [128, 1024], BF16, allow_name_mangling=True)) for i in range(2)]
        mask_sb = bent(nc.sbuf_tensor("mask_sb", [128, NTO * 256], BF16, allow_name_mangling=True))
        Pe_sb = [bent(nc.sbuf_tensor(f"Pe{i}", [128, 2048], BF16, allow_name_mangling=True)) for i in range(2)]
        Pt_sb = [bent(nc.sbuf_tensor(f"Pt{i}", [128, 2048], BF16, allow_name_mangling=True)) for i in range(3)]
        ksmp = bent(nc.sbuf_tensor("ksmp", [128, 128], BF16, allow_name_mangling=True))
        biasv = bent(nc.sbuf_tensor("biasv", [128, 2], F32, allow_name_mangling=True))
        zv = bent(nc.sbuf_tensor("zv", [128, 2], F32, allow_name_mangling=True))
        rz = bent(nc.sbuf_tensor("rz", [128, 2], F32, allow_name_mangling=True))
        Sps = bent(nc.psum_tensor("Sps", [128, 2048], F32, allow_name_mangling=True))
        smpps = bent(nc.psum_tensor("smpps", [128, 512], F32, allow_name_mangling=True))
        avps = bent(nc.psum_tensor("avps", [128, 512], F32, allow_name_mangling=True))
        trpB = [bent(nc.psum_tensor(f"trpB{i}", [128, 1024], BF16, allow_name_mangling=True)) for i in range(2)]

        p.dma("gpsimd", "tbl", mask_sb[:, :], maskc.ap())
        p.dma("sync", "qh0", qh[0][:, :], qdr.ap()[0:128, :])
        p.barrier()

        units = [(h, s) for h in range(NH) for s in range(NTO)]
        prev = None
        prev2 = []
        for u in range(len(units) + 2):
            cur = None
            if u < len(units):
                h, s = units[u]
                par = u % 2
                g = h // 4
                ext = 2 * s + 2
                if s == 0 and h + 1 < NH:
                    p.wr_wait("sync", ("qh", (h + 1) % 2))
                    mqh = p.dma("sync", f"qh{(h + 1) % 2}", qh[(h + 1) % 2][:, :],
                                qdr.ap()[(h + 1) * 128:(h + 2) * 128, :])
                    p.rd_mark(("qh_ld", (h + 1) % 2), f"qh{(h + 1) % 2}", mqh)
                if s == 0 and h % 4 == 0:
                    p.wr_wait("vector", "ksmp")
                    msm = p.do("vector", lambda E, g=g: E.tensor_copy(
                        ksmp[:, :], bass.AP(K_sb, g * S, [[NKV * S, 128], [16, 128]])),
                        inc="sV")
                    p.rd_mark("ksmp_w", "sV", msm)
                p.wait("tensor", "sV", p.war["ksmp_w"][0][1])
                if s == 0:
                    p.wr_wait("tensor", ("qh_ld", h % 2))
                p.wr_wait("tensor", "smp")
                msp = p.do("tensor", lambda E, h=h, s=s: E.matmul(
                    smpps[:, 0:128],
                    qh[h % 2][:, s * 128:(s + 1) * 128],
                    ksmp[:, :], start=True, stop=True), inc="sP")
                p.rd_mark("ksmp", "sP", msp)
                p.wait("vector", "sP", msp)
                p.wr_wait("vector", ("biasv", par))
                mb = p.do("vector", lambda E, par=par: E.reduce_max(
                    biasv[:, par:par + 1], smpps[:, 0:128],
                    axis=AX.X, negate=True), inc="sV")
                p.rd_mark("smp", "sV", mb)
                if prev is not None:
                    _ph, _ps, _ppar, _pext, _pmex = prev
                    p.wait("vector", "sC", _pmex)
                    p.wr_wait("vector", ("rz", _ppar))
                    _mrz = p.do("vector", lambda E, _ppar=_ppar: E.reciprocal(
                        rz[:, _ppar:_ppar + 1], zv[:, _ppar:_ppar + 1]), inc="sV")
                    p.rd_mark(("zv", _ppar), "sV", _mrz)
                    p.wait("vector", "sV", _mrz)
                    _pmmk = p.do("vector", lambda E, _pext=_pext, _ppar=_ppar: E.tensor_scalar(
                        Pe_sb[_ppar][:, 0:_pext * 128], Pe_sb[_ppar][:, 0:_pext * 128],
                        rz[:, _ppar:_ppar + 1], None, ALU.mult), inc="sV")
                    p.rd_mark(("rz", _ppar), "sV", _pmmk)
                    prev = (_ph, _ps, _ppar, _pext, _pmmk)
                soff = (par * 1024) if ext <= 8 else 0
                if ext <= 8:
                    p.wr_wait("tensor", ("Sh", par))
                else:
                    p.wr_wait("tensor", ("Sh", 0))
                    p.wr_wait("tensor", ("Sh", 1))
                nsl = (ext * 128 + 511) // 512
                for sl in range(nsl):
                    w = min(512, ext * 128 - sl * 512)
                    p.do("tensor", lambda E, h=h, s=s, g=g, sl=sl, w=w, soff=soff: E.matmul(
                        Sps[:, soff + sl * 512: soff + sl * 512 + w],
                        qh[h % 2][:, s * 128:(s + 1) * 128],
                        K_sb[:, g * S + sl * 512: g * S + sl * 512 + w],
                        start=True, stop=False))
                msc = p.do("tensor", lambda E, ext=ext, s=s, soff=soff: E.matmul(
                    Sps[:, soff + (ext - 2) * 128: soff + ext * 128],
                    ident[:, :],
                    mask_sb[:, s * 256:(s + 1) * 256],
                    start=False, stop=True), inc="sP")
                if s == NTO - 1:
                    p.rd_mark(("qh", h % 2), "sP", msc)
                p.wait("scalar", "sP", msc)
                p.wait("scalar", "sV", mb)
                p.wr_wait("scalar", ("Pe", par))
                p.wr_wait("scalar", ("zv", par))
                mex = p.do("scalar", lambda E, ext=ext, par=par, soff=soff: E.activation(
                    Pe_sb[par][:, 0:ext * 128], Sps[:, soff:soff + ext * 128], AF.Exp,
                    bias=biasv[:, par:par + 1],
                    accum_out=zv[:, par:par + 1]), inc="sC")
                if ext <= 8:
                    p.rd_mark(("Sh", par), "sC", mex)
                else:
                    p.rd_mark(("Sh", 0), "sC", mex)
                    p.rd_mark(("Sh", 1), "sC", mex)
                p.rd_mark(("biasv", par), "sC", mex)
                cur = (h, s, par, ext, mex)

            if prev is not None:
                ph, ps_, ppar, pext, pmmk = prev
                pu = u - 1
                if u >= len(units):
                    p.wait("vector", "sC", pmmk)
                    p.wr_wait("vector", ("rz", ppar))
                    _mrz = p.do("vector", lambda E, ppar=ppar: E.reciprocal(
                        rz[:, ppar:ppar + 1], zv[:, ppar:ppar + 1]), inc="sV")
                    p.rd_mark(("zv", ppar), "sV", _mrz)
                    p.wait("vector", "sV", _mrz)
                    pmmk = p.do("vector", lambda E, pext=pext, ppar=ppar: E.tensor_scalar(
                        Pe_sb[ppar][:, 0:pext * 128], Pe_sb[ppar][:, 0:pext * 128],
                        rz[:, ppar:ppar + 1], None, ALU.mult), inc="sV")
                    p.rd_mark(("rz", ppar), "sV", pmmk)
                pt3 = pu % 3
                mev = None
                for grp in range((pext + 7) // 8):
                    tb = grp % 2
                    n_in = min(8, pext - grp * 8)
                    p.wr_wait("tensor", ("trpB", tb))
                    p.wait("tensor", "sV", pmmk)
                    mt = None
                    for i in range(n_in):
                        ti = grp * 8 + i
                        mt = p.do("tensor", lambda E, ti=ti, i=i, tb=tb, ppar=ppar: E.transpose(
                            trpB[tb][:, i * 128:(i + 1) * 128],
                            Pe_sb[ppar][:, ti * 128:(ti + 1) * 128], ident[:, :]),
                            inc="sP" if i == n_in - 1 else None)
                    p.wait("vector", "sP", mt)
                    p.wr_wait("vector", ("Pt", pt3))
                    mev = p.do("vector", lambda E, tb=tb, pt3=pt3, grp=grp, n=n_in:
                               E.tensor_copy(Pt_sb[pt3][:, grp * 1024: grp * 1024 + n * 128],
                                             trpB[tb][:, 0:n * 128]), inc="sV")
                    p.rd_mark(("trpB", tb), "sV", mev)
                p.rd_mark(("Pe", ppar), "sP", p.cnt["sP"])
                prev2 = prev2 + [(ph, ps_, pext, pt3, mev)]
            if len(prev2) > (1 if u < len(units) + 1 else 0):
                ph2, ps2, pext2, pt32, mev2 = prev2.pop(0)
                pg2 = ph2 // 4
                p.wait("tensor", "sV", mev2)
                p.wr_wait("tensor", "av")
                mav = None
                for ti in range(pext2):
                    mav = p.do("tensor", lambda E, ti=ti, pg2=pg2, pt32=pt32, pext2=pext2: E.matmul(
                        avps[:, 0:128],
                        V_tm[:, ti * 1024 + pg2 * 128: ti * 1024 + (pg2 + 1) * 128],
                        Pt_sb[pt32][:, ti * 128:(ti + 1) * 128],
                        start=(ti == 0), stop=(ti == pext2 - 1)),
                        inc="sP" if ti == pext2 - 1 else None)
                p.rd_mark(("Pt", pt32), "sP", mav)
                p.wait("scalar", "sP", mav)
                mao = p.do("scalar", lambda E, ph2=ph2, ps2=ps2: E.activation(
                    AO[:, ph2 * 1024 + ps2 * 128: ph2 * 1024 + (ps2 + 1) * 128],
                    avps[:, 0:128], AF.Copy), inc="sC")
                p.rd_mark("av", "sC", mao)
            prev = cur

    if DEBUG:
        p.barrier()
        p.dma("sync", "qst", aodbg.ap()[:, :], AO[:, :])
        p.barrier()

    ctxAB.close()

    p.on = TRUNC >= 3
    p.barrier()
    ctxX3 = ExitStack()
    xn3 = ctxX3.enter_context(nc.sbuf_tensor("xn3", [128, KC * 1024], BF16, allow_name_mangling=True))
    with ExitStack() as cctx:
        cent = cctx.enter_context
        oacc = [cent(nc.psum_tensor(f"oacc{i}", [128, 512], F32, allow_name_mangling=True)) for i in range(2)]
        trpC = [cent(nc.psum_tensor(f"trpC{i}", [128, 1024], BF16, allow_name_mangling=True)) for i in range(2)]
        octx = ExitStack()
        cb = [octx.enter_context(nc.sbuf_tensor(f"ccb{i}", [128, KC * 128], BF16, allow_name_mangling=True)) for i in range(2)]
        sc = [octx.enter_context(nc.sbuf_tensor(f"csc{i}", [128, KC * 64], F32, allow_name_mangling=True)) for i in range(2)]
        wb = [octx.enter_context(nc.sbuf_tensor(f"cwb{i}", [128, KC * 128], BF16, allow_name_mangling=True)) for i in range(2)]
        ostg = [octx.enter_context(nc.sbuf_tensor(f"ostg{i}", [128, 512], BF16, allow_name_mangling=True)) for i in range(2)]
        otst = [octx.enter_context(nc.sbuf_tensor(f"otst{i}", [128, 512], BF16, allow_name_mangling=True)) for i in range(2)]

        mws = {0: emit_dequant_block(oc_, osc, D, 0, 0, cb, sc, wb)}
        for ocb in range(KC):
            parw = ocb % 2
            if ocb + 1 < KC:
                mws[ocb + 1] = emit_dequant_block(oc_, osc, D, ocb + 1, (ocb + 1) % 2, cb, sc, wb)
            mw = mws.pop(ocb)
            p.wait("tensor", "sV", mw)
            for tg in range(2):
                par = tg
                p.wr_wait("tensor", ("oacc", par))
                mp = None
                for ac in range(KC):
                    mp = p.do("tensor", lambda E, par=par, ac=ac, tg=tg, parw=parw: E.matmul(
                        oacc[par][:, :],
                        wb[parw][:, ac * 128:(ac + 1) * 128],
                        AO[:, ac * 1024 + tg * 512: ac * 1024 + (tg + 1) * 512],
                        start=(ac == 0), stop=(ac == KC - 1)),
                        inc="sP" if ac == KC - 1 else None)
                if tg == 1:
                    p.rd_mark(("wb", id(wb[parw])), "sP", mp)
                p.wait("scalar", "sP", mp)
                p.wr_wait("scalar", ("ostg", par))
                me = p.do("scalar", lambda E, par=par: E.activation(
                    ostg[par][:, :], oacc[par][:, :], AF.Copy), inc="sC")
                p.rd_mark(("oacc", par), "sC", me)
                tb = par
                p.wr_wait("tensor", ("trp", id(trpC[tb])))
                p.wait("tensor", "sC", me)
                mt = None
                for i in range(4):
                    mt = p.do("tensor", lambda E, i=i, tb=tb, par=par: E.transpose(
                        trpC[tb][:, i * 128:(i + 1) * 128],
                        ostg[par][:, i * 128:(i + 1) * 128], ident[:, :]),
                        inc="sP" if i == 3 else None)
                p.rd_mark(("ostg", par), "sP", mt)
                p.wait("vector", "sP", mt)
                p.wr_wait("vector", ("otst", par))
                mv2 = p.do("vector", lambda E, tb=tb, par=par: E.tensor_copy(
                    otst[par][:, :], trpC[tb][:, 0:512]), inc="sV")
                p.rd_mark(("trp", id(trpC[tb])), "sV", mv2)
                p.wait("sync", "sV", mv2)
                dest = bass.AP(otdr, (tg * 4 * 128) * D + ocb * 128,
                               [[D, 128], [128 * D, 4], [1, 128]])
                mo2 = p.dma("sync", f"ots{par}", dest, otst[par][:, :].rearrange(
                    "p (t c) -> p t c", c=128))
                p.rd_mark(("otst", par), f"ots{par}", mo2)

        ctxAO.close()
        octx.close()
        p.barrier()
        xot = [cent(nc.sbuf_tensor(f"xot{i}", [128, D], F32, allow_name_mangling=True)) for i in range(2)]
        otb = [cent(nc.sbuf_tensor(f"otb{i}", [128, D], BF16, allow_name_mangling=True)) for i in range(2)]
        t2b = cent(nc.sbuf_tensor("t2b", [128, D], BF16, allow_name_mangling=True))
        x1b = cent(nc.sbuf_tensor("x1b", [128, D], F32, allow_name_mangling=True))
        xn3tm = cent(nc.sbuf_tensor("xn3tm", [128, D], BF16, allow_name_mangling=True))
        for tl in range(NTO):
            xpar = tl % 2
            p.wr_wait("sync", ("xot", xpar))
            mx = p.dma("sync", f"xo{xpar}", xot[xpar][:, :], xo.ap()[tl * 128:(tl + 1) * 128, :])
            p.wr_wait("sync", ("otb", xpar))
            mot = p.dma("sync", f"otb{xpar}", otb[xpar][:, :], otdr.ap()[tl * 128:(tl + 1) * 128, :])
            p.wr_wait("vector", "t2b")
            mln2 = emit_ln_apply(otb[xpar][:, :], t2b[:, :], [(f"otb{xpar}", mot)])
            p.rd_mark(("otb", xpar), "sV", mln2)
            p.wait("vector", f"xo{xpar}", mx)
            p.wr_wait("vector", "x1b")
            mx1 = p.do("vector", lambda E, xpar=xpar: E.tensor_tensor(
                x1b[:, :], xot[xpar][:, :], t2b[:, :], ALU.add), inc="sV")
            p.rd_mark(("xot", xpar), "sV", mx1)
            p.rd_mark("t2b", "sV", mx1)
            p.wait("sync", "sV", mx1)
            ms1 = p.dma("sync", "x1s", x1dr.ap()[tl * 128:(tl + 1) * 128, :], x1b[:, :])
            p.wr_wait("vector", "xn3tm")
            mln3 = emit_ln_apply(x1b[:, :], xn3tm[:, :])
            p.rd_mark("x1b", "sV", mln3)
            p.rd_mark("x1b", "x1s", ms1, replace=False)
            emit_transpose_tile(xn3tm, mln3, trpC, xn3, 1024, tl * 128, "xn3", evac_scalar=True)
            p.rd_mark("xn3tm", "sP", p.cnt["sP"])

    p.on = TRUNC >= 4
    p.barrier()
    with ExitStack() as dctx:
        dent = dctx.enter_context
        b1_sb = dent(nc.sbuf_tensor("b1_sb", [128, FC], F32, allow_name_mangling=True))
        w1b = [dent(nc.sbuf_tensor(f"w1b{i}", [128, KC * 128], BF16, allow_name_mangling=True)) for i in range(2)]
        hstg = [dent(nc.sbuf_tensor(f"hstg{i}", [128, 512], BF16, allow_name_mangling=True)) for i in range(2)]
        hps = [dent(nc.psum_tensor(f"hps{i}", [128, 512], F32, allow_name_mangling=True)) for i in range(2)]
        mb1 = p.dma("sync", "tbl", b1_sb[:, :], b1.ap())
        p.wait("scalar", "tbl", mb1)
        for fc in range(FC):
            parw = fc % 2
            p.wr_wait("gpsimd", ("w1b", parw))
            mwd = p.dma("gpsimd", f"w1{parw}",
                        w1b[parw][:, :],
                        bass.AP(w1, fc * (128 * D), [[D, 128], [1, D]]))
            p.wait("tensor", f"w1{parw}", mwd)
            for tg in range(2):
                par = tg
                p.wr_wait("tensor", ("hps", par))
                mp = None
                for kch in range(KC):
                    mp = p.do("tensor", lambda E, par=par, parw=parw, kch=kch, tg=tg: E.matmul(
                        hps[par][:, :],
                        w1b[parw][:, kch * 128:(kch + 1) * 128],
                        xn3[:, kch * 1024 + tg * 512: kch * 1024 + (tg + 1) * 512],
                        start=(kch == 0), stop=(kch == KC - 1)),
                        inc="sP" if kch == KC - 1 else None)
                if tg == 1:
                    p.rd_mark(("w1b", parw), "sP", mp)
                p.wait("scalar", "sP", mp)
                p.wr_wait("scalar", ("hstg", par))
                mg = p.do("scalar", lambda E, par=par, fc=fc: E.activation(
                    hstg[par][:, :], hps[par][:, :], AF.Gelu_apprx_tanh,
                    bias=b1_sb[:, fc:fc + 1]), inc="sC")
                p.rd_mark(("hps", par), "sC", mg)
                p.wait("sync", "sC", mg)
                mh = p.dma("sync", f"hs{par}",
                           hdr.ap()[fc * 128:(fc + 1) * 128, tg * 512:(tg + 1) * 512],
                           hstg[par][:, :])
                p.rd_mark(("hstg", par), f"hs{par}", mh)
    ctxX3.close()

    p.on = TRUNC >= 5
    p.barrier()
    FT = ent(nc.sbuf_tensor("FT", [128, NTO * D], BF16, allow_name_mangling=True))
    bnF = ent(nc.sbuf_tensor("bnF", [128, NTO * KC * 6], F32, allow_name_mangling=True))
    with ExitStack() as ectx:
        eent = ectx.enter_context
        b2_sb = eent(nc.sbuf_tensor("b2_sb", [128, KC], F32, allow_name_mangling=True))
        w2b = [eent(nc.sbuf_tensor(f"w2b{i}", [128, 4 * 512], BF16, allow_name_mangling=True)) for i in range(2)]
        hb = [eent(nc.sbuf_tensor(f"hb{i}", [128, 1024], BF16, allow_name_mangling=True)) for i in range(4)]
        festg = eent(nc.sbuf_tensor("festg", [128, D], BF16, allow_name_mangling=True))
        mb2 = p.dma("sync", "tbl", b2_sb[:, :], b2.ap())
        prev_trpE_done = None
        for dg in range(8):
            with ExitStack() as psctx:
                acc = [psctx.enter_context(nc.psum_tensor(f"eacc{dg}_{i}", [128, 1024], F32, allow_name_mangling=True))
                       for i in range(4)]
                mwd = None
                for fc in range(FC):
                    if fc % 4 == 0:
                        parw = (fc // 4) % 2
                        p.wr_wait("gpsimd", ("w2b", parw))
                        mwd = p.dma("gpsimd", f"w2{parw}",
                                    w2b[parw][:, :],
                                    bass.AP(w2, ((dg * 32 + fc // 4) * 128) * 2048,
                                            [[2048, 128], [1, 2048]]))
                        wsem = f"w2{parw}"
                    parh = fc % 4
                    p.wr_wait("sync", ("hb", parh))
                    mhl = p.dma("sync", f"hb{parh}", hb[parh][:, :],
                                hdr.ap()[fc * 128:(fc + 1) * 128, :])
                    p.wait("tensor", wsem, mwd)
                    p.wait("tensor", f"hb{parh}", mhl)
                    if fc == 0 and prev_trpE_done is not None:
                        p.wait("tensor", "sV", prev_trpE_done)
                    mp = None
                    for dc in range(4):
                        for tg in range(2):
                            mp = p.do("tensor", lambda E, dc=dc, tg=tg, parw=parw, fc=fc, parh=parh: E.matmul(
                                acc[dc][:, tg * 512:(tg + 1) * 512],
                                w2b[parw][:, (fc % 4) * 512 + dc * 128: (fc % 4) * 512 + (dc + 1) * 128],
                                hb[parh][:, tg * 512:(tg + 1) * 512],
                                start=(fc == 0), stop=(fc == FC - 1)),
                                inc="sP" if (dc == 3 and tg == 1) else None)
                    p.rd_mark(("hb", parh), "sP", mp)
                    if fc % 4 == 3:
                        p.rd_mark(("w2b", parw), "sP", mp)
                p.wait("vector", "sP", mp)
                p.wait("vector", "tbl", mb2)
                p.wr_wait("vector", "festg")
                mev = None
                for dc in range(4):
                    for tg in range(2):
                        mev = p.do("vector", lambda E, dc=dc, tg=tg, dg=dg: E.tensor_scalar(
                            festg[:, (dc * 2 + tg) * 512:(dc * 2 + tg + 1) * 512],
                            acc[dc][:, tg * 512:(tg + 1) * 512],
                            b2_sb[:, dg * 4 + dc: dg * 4 + dc + 1], None, ALU.add),
                            inc="sV")
            with ExitStack() as psctx2:
                trpE = [psctx2.enter_context(
                    nc.psum_tensor(f"trpE{dg}_{i}", [128, 1024], BF16, allow_name_mangling=True)) for i in range(2)]
                mv2 = None
                for dc in range(4):
                    tb = dc % 2
                    p.wr_wait("tensor", ("trpE", tb))
                    p.wait("tensor", "sV", mev)
                    mt = None
                    for i in range(8):
                        tg = i // 4
                        i2 = i % 4
                        mt = p.do("tensor", lambda E, dc=dc, tg=tg, i2=i2, i=i, tb=tb: E.transpose(
                            trpE[tb][:, i * 128:(i + 1) * 128],
                            festg[:, (dc * 2 + tg) * 512 + i2 * 128:
                                  (dc * 2 + tg) * 512 + (i2 + 1) * 128],
                            ident[:, :]), inc="sP" if i == 7 else None)
                    if dc == 3:
                        p.rd_mark("festg", "sP", mt)
                    p.wait("vector", "sP", mt)
                    dest = bass.AP(FT, (dg * 4 + dc) * 128, [[NTO * D, 128], [D, 8], [1, 128]])
                    mv2 = p.do("vector", lambda E, d=dest, tb=tb: E.tensor_copy(
                        d, trpE[tb][:, :]), inc="sV")
                    p.rd_mark(("trpE", tb), "sV", mv2)
                    ch = dg * 4 + dc
                    for tl in range(NTO):
                        p.do("vector", lambda E, tl=tl, ch=ch: E.bn_stats(
                            bnF[:, (tl * KC + ch) * 6:(tl * KC + ch) * 6 + 6],
                            FT[:, tl * D + ch * 128: tl * D + (ch + 1) * 128]))
                prev_trpE_done = mv2

    p.on = TRUNC >= 6
    p.barrier()
    with ExitStack() as fctx:
        fent = fctx.enter_context
        x1t = [fent(nc.sbuf_tensor(f"x1t{i}", [128, D], F32, allow_name_mangling=True)) for i in range(2)]
        t4b = fent(nc.sbuf_tensor("t4b", [128, D], BF16, allow_name_mangling=True))
        outb = [fent(nc.sbuf_tensor(f"outb{i}", [128, D], F32, allow_name_mangling=True)) for i in range(2)]
        for tl in range(NTO):
            xpar = tl % 2
            p.wr_wait("sync", ("x1t", xpar))
            mx = p.dma("sync", f"x1t{xpar}", x1t[xpar][:, :], x1dr.ap()[tl * 128:(tl + 1) * 128, :])
            p.wr_wait("vector", "t4b")
            p.wr_wait("vector", "mvv")
            m2 = p.do("vector", lambda E, tl=tl: E.bn_aggr(
                mvv[:, :2], bnF[:, tl * KC * 6:(tl + 1) * KC * 6]), inc="sV")
            p.wait("scalar", "sV", m2)
            p.wr_wait("scalar", "stdv")
            m3 = p.do("scalar", lambda E: E.activation(
                stdv[:, :1], mvv[:, 1:2], AF.Sqrt, bias=epsv[:, :1]), inc="sC")
            p.wait("vector", "sC", m3)
            p.wr_wait("vector", "rstdv")
            m4 = p.do("vector", lambda E: E.reciprocal(rstdv[:, :1], stdv[:, :1]), inc="sV")
            p.rd_mark("stdv", "sV", m4)
            p.wait("vector", "sV", m4)
            mln4 = p.do("vector", lambda E, tl=tl: E.tensor_scalar(
                t4b[:, :], FT[:, tl * D:(tl + 1) * D], mvv[:, 0:1], rstdv[:, :1],
                ALU.subtract, ALU.mult), inc="sV")
            p.rd_mark("mvv", "sV", mln4)
            p.rd_mark("rstdv", "sV", mln4)
            p.wait("vector", f"x1t{xpar}", mx)
            p.wr_wait("vector", ("outb", xpar))
            mo = p.do("vector", lambda E, xpar=xpar: E.tensor_tensor(
                outb[xpar][:, :], x1t[xpar][:, :], t4b[:, :], ALU.add), inc="sV")
            p.rd_mark(("x1t", xpar), "sV", mo)
            p.rd_mark("t4b", "sV", mo)
            p.wait("sync", "sV", mo)
            mst = p.dma("sync", f"os{xpar}", out.ap()[tl * 128:(tl + 1) * 128, :], outb[xpar][:, :])
            p.rd_mark(("outb", xpar), f"os{xpar}", mst)

    p.on = True
    for sname in p.cnt:
        p.wait("sync", sname)
        p.wait("gpsimd", sname)

    block = ctx.enter_context(nc.Block())
    p.finalize(block)
    ctx.close()
    return nc


def _rope_perm(width):
    nh = width // HD
    perm = []
    for h in range(nh):
        base = h * HD
        perm.extend([base + 2 * i for i in range(64)])
        perm.extend([base + 2 * i + 1 for i in range(64)])
    return np.array(perm, dtype=np.int64)


def _host_tables():
    pos = np.arange(S, dtype=np.float64)
    inv = 1.0 / (10000.0 ** (np.arange(0, HD, 2, dtype=np.float64) / HD))
    f = pos[:, None] * inv[None, :]
    cos = np.cos(f).T.astype(np.float32)
    sin = np.sin(f).T.astype(np.float32)
    return cos, sin


def kernel(L0_attn_hidden_states, L0_attn_q_codes, L0_attn_q_scales,
           L0_attn_k_codes, L0_attn_k_scales, L0_attn_v_codes, L0_attn_v_scales,
           L0_attn_o_codes, L0_attn_o_scales, L0_ffn_inter_w, L0_ffn_inter_b,
           L0_ffn_output_w, L0_ffn_output_b):
    x = np.ascontiguousarray(np.asarray(L0_attn_hidden_states, dtype=np.float32))
    B = x.shape[0]
    permq = _rope_perm(D)
    permk = _rope_perm(NKV * HD)
    def codes_layout(arr, W):
        return np.ascontiguousarray(
            arr.reshape(KC, 128, W // 128, 128).transpose(2, 1, 0, 3).reshape(W, KC * 128))

    def scales_layout(arr, W):
        return np.ascontiguousarray(
            arr.reshape(KC, W // 128, 128).transpose(1, 0, 2).reshape(W // 128, KC * 128))

    qcod = codes_layout(np.asarray(L0_attn_q_codes, dtype=np.int32)[:, permq], D)
    qsc = scales_layout(np.asarray(L0_attn_q_scales, dtype=np.float32)[:, permq], D)
    kcod = codes_layout(np.asarray(L0_attn_k_codes, dtype=np.int32)[:, permk], NKV * HD)
    ksc = scales_layout(np.asarray(L0_attn_k_scales, dtype=np.float32)[:, permk], NKV * HD)
    vcod = codes_layout(np.asarray(L0_attn_v_codes, dtype=np.int32), NKV * HD)
    vsc = scales_layout(np.asarray(L0_attn_v_scales, dtype=np.float32), NKV * HD)
    ocod = codes_layout(np.asarray(L0_attn_o_codes, dtype=np.int32), D)
    osc = scales_layout(np.asarray(L0_attn_o_scales, dtype=np.float32), D)
    w1 = np.ascontiguousarray(np.asarray(L0_ffn_inter_w, dtype=np.float32)
                              .reshape(KC, 128, FC, 128).transpose(2, 1, 0, 3).reshape(FF, D))
    b1 = np.ascontiguousarray(np.asarray(L0_ffn_inter_b, dtype=np.float32).reshape(FC, 128).T)
    w2 = np.ascontiguousarray(np.asarray(L0_ffn_output_w, dtype=np.float32)
                              .reshape(32, 4, 128, 8, 512).transpose(3, 0, 2, 1, 4).reshape(FF, D))
    b2 = np.ascontiguousarray(np.asarray(L0_ffn_output_b, dtype=np.float32).reshape(KC, 128).T)
    cos, sin = _host_tables()
    ident = np.eye(128, dtype=np.float32)

    tri = np.tril(np.ones((128, 128), dtype=np.float32))
    zeros = np.zeros((128, 128), dtype=np.float32)
    ones = np.ones((128, 128), dtype=np.float32)

    in_maps = []
    for c in range(8):
        b = c // 2
        j = c % 2
        own_tiles = [2 * s + j for s in range(NTO)]
        xb = x[b]
        xown = np.ascontiguousarray(xb.reshape(NT, 128, D)[own_tiles].reshape(OWN, D))
        own_pos = np.concatenate([np.arange(t * 128, (t + 1) * 128) for t in own_tiles])
        cq = np.ascontiguousarray(np.vstack([cos[:, own_pos], cos[:, own_pos]]) * np.float32(HD ** -0.5))
        sq = np.ascontiguousarray(np.vstack([sin[:, own_pos], sin[:, own_pos]]) * np.float32(HD ** -0.5))
        NEG = np.float32(-1e30)
        tri_add = np.where(tri > 0, np.float32(0.0), NEG)
        mk = np.empty((128, NTO * 256), dtype=np.float32)
        for s in range(NTO):
            if j == 0:
                mk[:, s * 256:s * 256 + 128] = tri_add
                mk[:, s * 256 + 128:(s + 1) * 256] = NEG
            else:
                mk[:, s * 256:s * 256 + 128] = 0.0
                mk[:, s * 256 + 128:(s + 1) * 256] = tri_add
        in_maps.append({
            "xf": xb, "xo": xown,
            "qc": qcod, "qsc": qsc, "kc": kcod, "ksc": ksc,
            "vc": vcod, "vsc": vsc, "oc": ocod, "osc": osc,
            "w1": w1, "b1": b1, "w2": w2, "b2": b2,
            "cosq": cq, "sinq": sq,
            "cosk": np.ascontiguousarray(np.vstack([cos, cos])),
            "sink": np.ascontiguousarray(np.vstack([sin, sin])),
            "maskc": mk, "identf": ident,
        })

    ncores = int(os.environ.get("MK_CORES", "8"))
    nc = build_nc()
    res = run_bass_kernel_spmd(nc, in_maps[:ncores], core_ids=list(range(ncores)))

    outx = np.zeros((B, S, D), dtype=np.float32)
    for c in range(ncores):
        b = c // 2
        j = c % 2
        own_tiles = [2 * s + j for s in range(NTO)]
        o = np.asarray(res.results[c]["out"], dtype=np.float32).reshape(NTO, 128, D)
        view = outx[b].reshape(NT, 128, D)
        for i, t in enumerate(own_tiles):
            view[t] = o[i]
    return outx
